# version 22
# baseline (speedup 1.0000x reference)
import contextlib
import numpy as np
import ml_dtypes
import concourse.bass as bass
import concourse.mybir as mybir
from concourse.bass_utils import run_bass_kernel_spmd

F32 = mybir.dt.float32
BF16 = mybir.dt.bfloat16
U8 = mybir.dt.uint8
AF = mybir.ActivationFunctionType
ALU = mybir.AluOpType
AX = mybir.AxisListType

B, N, D = 4, 4096, 1024
NCORES = 8
T = 2048
NT = T // 128
NCH = D // 128
HEADS, DH, WINDOW = 8, 64, 512
HID, HID2 = 2730, 5460
NJ = 22
QK_SCALE = 8.0
KH = 16
HALO = 512
HALOT = HALO + 1
EPS = 1e-5
BIG = 1e30

TAPS = []
BUILD_UPTO = "full"



def _rotary_tables(pos):
    half = np.arange(0, DH, 2, dtype=np.float64)
    inv_freq = 1.0 / 10000.0 ** (half / DH)
    freqs = pos[:, None].astype(np.float64) * inv_freq[None, :]
    freqs = np.concatenate([freqs, freqs], -1)
    base = (half + 0.4 * DH) / (1.4 * DH)
    power = (pos.astype(np.float64) - N // 2) / float(WINDOW // 2)
    scale = base[None, :] ** power[:, None]
    scale = np.concatenate([scale, scale], -1)
    return np.cos(freqs), np.sin(freqs), scale


def _host_tables(half):
    own0 = half * T
    pos = np.concatenate([np.arange(own0 - HALO, own0), np.arange(own0, own0 + T)])
    pos = np.maximum(pos, 0)
    cos, sin, scale = _rotary_tables(pos)
    sgn = np.concatenate([-np.ones(32), np.ones(32)])[None, :]
    cq = (cos * scale * QK_SCALE).astype(np.float32)
    sq = (sgn * sin * scale * QK_SCALE).astype(np.float32)
    ck = (cos / scale).astype(np.float32)
    sk = (sgn * sin / scale).astype(np.float32)

    def chunk_mask(qc, kc):
        m = np.full((128, 128), -1e9, np.float32)
        qpos = own0 + qc * 128 + np.arange(128)[:, None]
        kpos = own0 + kc * 128 + np.arange(128)[None, :]
        ok = (qpos - kpos >= 0) & (qpos - kpos < WINDOW) & (kpos >= 0)
        m[ok] = 0.0
        return m

    mask_std = np.concatenate([chunk_mask(8, 8), chunk_mask(8, 4),
                               np.zeros((128, 384), np.float32)], axis=1)
    mask_lo = np.zeros((4, 128, 640), np.float32)
    for qc in range(4):
        chunks = [qc, qc - 4, qc - 3, qc - 2, qc - 1]
        mask_lo[qc] = np.concatenate([chunk_mask(qc, kc) for kc in chunks], axis=1)

    return {
        "cq": cq, "sq": sq, "ck": ck, "sk": sk,
        "mask_std": mask_std, "mask_lo": mask_lo.reshape(4 * 128, 640),
        "flagcol": np.full((128, 1), float(half), np.float32),
        "ident": np.eye(128, dtype=ml_dtypes.bfloat16),
        "identf": np.eye(128, dtype=np.float32),
    }



ENGINES = ("sync", "scalar", "vector", "tensor", "gpsimd")
_PSUM_BANK = 2048
_DT_SIZE = {F32: 4, BF16: 2, U8: 1, mybir.dt.uint32: 4, mybir.dt.float16: 2}


class SB:

    def __init__(self, nc, stack):
        self.nc = nc
        self.eng = {"sync": nc.sync, "scalar": nc.scalar, "vector": nc.vector,
                    "tensor": nc.tensor, "gpsimd": nc.gpsimd}
        self.sem = {}
        self.cnt = {}
        NDMASEM = 5
        self.dpool = {}
        self.didx = {}
        for e in ENGINES:
            name = "c_" + e
            self.sem[name] = stack.enter_context(nc.semaphore(name))
            self.cnt[name] = 0
            self.dpool[e] = []
            self.didx[e] = 0
            for i in range(NDMASEM):
                name = f"d_{e}{i}"
                self.sem[name] = stack.enter_context(nc.semaphore(name))
                self.cnt[name] = 0
                self.dpool[e].append(name)
        self.sem["cc"] = stack.enter_context(nc.semaphore("cc"))
        self.cnt["cc"] = 0
        self.clock = {e: {} for e in ENGINES}
        self.snap = {}
        self.acc = {}
        self.bank = {}
        self.psum_base = {}
        self.nwaits = 0
        self.nops = 0

    def _boxes(self, ap):
        th = ap.tensor
        name = th.name
        space = str(ap.space)
        esz = _DT_SIZE.get(ap.dtype, 4)
        dims = [list(d) for d in ap.ap]
        off = int(ap.offset)
        if space.endswith("DRAM"):
            if len(dims) == 2 and dims[1][0] == 1 and dims[0][0] >= dims[1][1]:
                rowlen = dims[0][0]
                p0 = off // rowlen
                c0 = off % rowlen
                return (name, p0, p0 + dims[0][1], c0 * esz,
                        (c0 + dims[1][1]) * esz, "D")
            return (name, 0, 1 << 30, 0, 1 << 60, "D")
        sp = "P" if space.endswith("PSUM") else "S"
        step0 = dims[0][0] if dims else 1
        if step0 <= 0 or len(dims) < 1:
            return (name, 0, 128, 0, 1 << 60, sp)
        p0 = off // step0
        c0 = off - p0 * step0
        span = 1
        for st, cntd in dims[1:]:
            span += abs(st) * (cntd - 1)
        if span > step0:
            return (name, p0, p0 + dims[0][1], 0, 1 << 60, sp)
        return (name, p0, p0 + dims[0][1], c0 * esz, (c0 + span) * esz, sp)

    @staticmethod
    def _overlap(a, b):
        return (a[1] < b[2] and b[1] < a[2] and a[3] < b[4] and b[3] < a[4])

    @staticmethod
    def _covers(a, b):
        return (a[1] <= b[1] and a[2] >= b[2] and a[3] <= b[3] and a[4] >= b[4])

    def _need(self, ename, deps, token):
        sem, val = token
        if self.clock[ename].get(sem, 0) >= val:
            return
        deps[sem] = max(deps.get(sem, 0), val)

    def _collect(self, ename, rboxes, wboxes):
        deps = {}
        for box in rboxes:
            for kind, b2, tok, eng2 in reversed(self.acc.get(box[0], ())):
                if kind == "w" and self._overlap(box, b2):
                    if not (eng2 == ename == "tensor"):
                        self._need(ename, deps, tok)
                    if self._covers(b2, box):
                        break
        for box in wboxes:
            for kind, b2, tok, eng2 in reversed(self.acc.get(box[0], ())):
                if self._overlap(box, b2):
                    if not (eng2 == ename == "tensor"):
                        self._need(ename, deps, tok)
                    if kind == "w" and self._covers(b2, box):
                        break
        return deps

    def _banks(self, box):
        base = self.psum_base.get(box[0])
        if base is None:
            try:
                base = int(self.nc.lookup_mloc(box[0]).bank)
            except Exception:
                base = 0
            self.psum_base[box[0]] = base
        b1 = max(box[3], box[4] - 1)
        return range(base + box[3] // _PSUM_BANK, base + b1 // _PSUM_BANK + 1)

    def op(self, ename, emit, reads=(), writes=(), kind="c"):
        eng = self.eng[ename]
        rb = [self._boxes(a) for a in reads]
        wb = [self._boxes(a) for a in writes]
        deps = self._collect(ename, rb, wb)
        for box in rb + wb:
            if box[5] == "P":
                for bk in self._banks(box):
                    last = self.bank.get(bk)
                    if last is not None and last[0] != ename:
                        self._need(ename, deps, last[1])
        if kind == "d":
            sname = self.dpool[ename][self.didx[ename] % len(self.dpool[ename])]
            self.didx[ename] += 1
            if self.cnt[sname]:
                self._need(ename, deps, (sname, self.cnt[sname]))
        elif kind == "cc":
            sname = "cc"
            if self.cnt[sname]:
                self._need(ename, deps, (sname, self.cnt[sname]))
        for sem, val in sorted(deps.items()):
            eng.wait_ge(self.sem[sem], val)
            self.nwaits += 1
            cl = self.clock[ename]
            cl[sem] = max(cl.get(sem, 0), val)
            for s2, v2 in self.snap.get((sem, val), {}).items():
                cl[s2] = max(cl.get(s2, 0), v2)
        inst = emit(eng)
        self.nops += 1
        if kind == "d":
            step = 16
        elif kind == "cc":
            step = 1
        else:
            sname, step = "c_" + ename, 1
        self.cnt[sname] += step
        val = self.cnt[sname]
        inst.then_inc(self.sem[sname], step)
        token = (sname, val)
        snapshot = dict(self.clock[ename])
        snapshot[sname] = val
        self.snap[token] = snapshot
        for box in rb:
            self.acc.setdefault(box[0], []).append(("r", box, token, ename))
        for box in wb:
            self.acc.setdefault(box[0], []).append(("w", box, token, ename))
        for box in rb + wb:
            if box[5] == "P":
                for bk in self._banks(box):
                    self.bank[bk] = (ename, token)
        return token

    def dma(self, ename, out, in_, **kw):
        return self.op(ename, lambda e: e.dma_start(out=out, in_=in_, **kw),
                       reads=[in_], writes=[out], kind="d")

    def drain_all(self):
        for sname, val in self.cnt.items():
            if val > 0:
                self.eng["sync"].wait_ge(self.sem[sname], val)



class Arena:
    def __init__(self, tensor, nbytes):
        self.t = tensor
        self.nbytes = nbytes
        self.off = 0
        self.hi = 0
        self.marks = []

    def push(self):
        self.marks.append(self.off)

    def pop(self):
        self.off = self.marks.pop()

    def take(self, shape, dt):
        esz = _DT_SIZE[dt]
        n = int(np.prod(shape[1:])) * esz
        off = (self.off + 127) // 128 * 128
        assert off + n <= self.nbytes, f"arena overflow: {off}+{n} > {self.nbytes}"
        self.off = off + n
        self.hi = max(self.hi, self.off)
        v = self.t[:, off:off + n].bitcast(dt)
        if len(shape) == 2:
            return v
        kw = {chr(ord('a') + i): shape[i + 1] for i in range(len(shape) - 2)}
        pat = "p (" + " ".join(chr(ord('a') + i) for i in range(len(shape) - 1)) + ") -> p " + \
              " ".join(chr(ord('a') + i) for i in range(len(shape) - 1))
        return v.rearrange(pat, **kw)



class Build:
    def __init__(self):
        self.stack = contextlib.ExitStack()
        self.nc = bass.Bass("TRN2", target_bir_lowering=False)
        self.sb = SB(self.nc, self.stack)
        self.tap_t = {}

    def sbuf(self, name, shape, dt):
        return self.stack.enter_context(self.nc.sbuf_tensor(name, shape, dt))

    def psumt(self, name, shape, dt=F32):
        return self.stack.enter_context(self.nc.psum_tensor(name, shape, dt))

    def tap_sbuf(self, name, ap):
        if name not in TAPS:
            return
        if callable(ap):
            ap = ap()
        t = self.nc.dram_tensor("tap_" + name, list(ap.shape), ap.dtype,
                                kind="ExternalOutput")
        self.sb.dma("sync", t.ap(), ap)
        self.tap_t[name] = t

    def build(self):
        nc = self.nc
        dr = {}
        def inp(name, shape, dt=F32):
            dr[name] = nc.dram_tensor(name, shape, dt, kind="ExternalInput")
            return dr[name]
        inp("x0", [T, D])
        inp("Wd", [3072, D])
        inp("wqkv", [2, D, 1536])
        inp("qs", [2, DH])
        inp("ks", [2, DH])
        inp("wo", [2, 512, D])
        inp("lng", [2, D])
        inp("lnb", [2, D])
        inp("w1", [2, D, HID2])
        inp("b1", [2, HID2])
        inp("w2", [2, HID, D])
        inp("b2", [2, D])
        inp("cq", [HALO + T, DH])
        inp("sq", [HALO + T, DH])
        inp("ck", [HALO + T, DH])
        inp("sk", [HALO + T, DH])
        inp("mask_std", [128, 640])
        inp("mask_lo", [4 * 128, 640])
        inp("flagcol", [128, 1])
        inp("ident", [128, 128], BF16)
        inp("identf", [128, 128], F32)
        self.dr = dr
        self.out = nc.dram_tensor("out", [T, D], F32, kind="ExternalOutput")

        dm = {}
        dm["x0cm"] = nc.dram_tensor("x0cm", [D, T], F32)
        dm["x1cm"] = nc.dram_tensor("x1cm", [D, T], F32)
        dm["x2cm"] = nc.dram_tensor("x2cm", [D, T], F32)
        dm["Ocm"] = nc.dram_tensor("Ocm", [D, T], F32)
        dm["Olcm"] = nc.dram_tensor("Olcm", [512, T], BF16)
        dm["s_w1"] = nc.dram_tensor("s_w1", [2, D, HID2], BF16)
        dm["s_w2"] = nc.dram_tensor("s_w2", [2, HID, D], BF16)
        dm["s_Wd"] = nc.dram_tensor("s_Wd", [3072, D], BF16)
        dm["cc1_in"] = nc.dram_tensor("cc1_in", [D, HALOT], BF16)
        dm["cc1_out"] = nc.dram_tensor("cc1_out", [2 * D, HALOT], BF16)
        dm["cc2_in"] = nc.dram_tensor("cc2_in", [128, NCH], F32)
        dm["cc2_out"] = nc.dram_tensor("cc2_out", [256, NCH], F32)
        dm["y2cm"] = nc.dram_tensor("y2cm", [D, T], BF16)
        dm["y2full"] = nc.dram_tensor("y2full", [2 * D, T], BF16)
        self.dm = dm

        self.xt = self.sbuf("xt", [128, NCH, 1 + T], F32)
        self.xthalo = self.sbuf("xthalo", [128, NCH, HALOT], BF16)
        self.rows = self.sbuf("rows", [1, 2, 4, 512], F32)
        self.browones = self.sbuf("browones", [2, 2, 512], F32)
        self.onesrow0 = self.sbuf("onesrow0", [1, 128], F32)
        self.mask_std = self.sbuf("mask_std_sb", [128, 640], F32)
        self.mask_lo = self.sbuf("mask_lo_sb", [128, 4, 640], F32)
        self.flag = self.sbuf("flag", [128, 1], F32)
        self.ident = self.sbuf("identsb", [128, 128], BF16)
        self.identf = self.sbuf("identfsb", [128, 128], F32)
        self.onescol = self.sbuf("onescol", [128, 1], F32)
        self.epsc = self.sbuf("epsc", [1, 1], F32)
        self.epscol = self.sbuf("epscol", [128, 1], F32)
        self.lgbc = self.sbuf("lgbc", [128, 2, 2, NCH], F32)
        self.b1c = self.sbuf("b1c", [128, 2, 2 * NJ], F32)
        self.b2c = self.sbuf("b2c", [128, 2, NCH], F32)
        self.bstg2 = self.sbuf("bstg2", [NJ, 2, 2, 128], F32)
        self.bstg3 = self.sbuf("bstg3", [NCH, 2, 128], F32)
        self.qsr = self.sbuf("qsr", [1, 2, DH], F32)
        self.ksr = self.sbuf("ksr", [1, 2, DH], F32)
        self.qsb = self.sbuf("qsb", [128, 4, DH], F32)
        self.negm = self.sbuf("negm", [128, 2], F32)
        self.stat = self.sbuf("stat", [128, 2048], F32)
        self.Pq = self.sbuf("Pq", [128, 3, 640], BF16)
        self.PT = self.sbuf("PT", [128, 2, 640], BF16)
        art = self.sbuf("arena", [128, 83968], U8)
        self.ar = Arena(art, 83968)

        self.pD = self.psumt("pD", [128, 2048], F32)
        self.ps4 = self.psumt("ps4", [128, 512], F32)
        self.ps5 = self.psumt("ps5", [128, 512], F32)
        self.pbf = self.psumt("pbf", [128, 1024], BF16)
        self.prow = self.psumt("prow", [128, 512], F32)

        self._consts()
        self._weights()
        self.ph0()
        self.tap_sbuf("y1", lambda: self.xt.ap())
        stages = {"ph0": 0, "l0": 1, "inv1": 2, "x1": 3, "dense": 4, "x2": 5,
                  "inv2": 6, "full": 7}[BUILD_UPTO]
        if stages >= 1:
            for l in range(min(stages, 2)):
                self.ffn_layer(1, l)
        if stages >= 3:
            self.combine(self.dm["x0cm"], self.dm["x1cm"], 0.5, y_dst="y2cm")
            self.cc3()
        if stages >= 4:
            self.dense_attn()
        if stages >= 5:
            self.combine(self.dm["x1cm"], self.dm["x2cm"], 1.0,
                         osrc=self.dm["Ocm"], y_dst="xt")
        if stages >= 6:
            for l in range(2):
                self.ffn_layer(2, l, emit_final=(stages >= 7 and l == 1))
        if stages >= 7:
            pass
        else:
            self._emit_out_from_xt()
        self.sb.drain_all()
        self.stack.close()
        return self

    def _consts(self):
        sb, dr = self.sb, self.dr
        sb.op("vector", lambda e: e.memset(self.bstg2.ap(), 0.0),
              writes=[self.bstg2.ap()])
        sb.op("vector", lambda e: e.memset(self.bstg3.ap(), 0.0),
              writes=[self.bstg3.ap()])
        sb.dma("sync", self.flag[:, :], dr["flagcol"][:, :])
        sb.dma("sync", self.ident[:, :], dr["ident"][:, :])
        sb.dma("sync", self.identf[:, :], dr["identf"][:, :])
        sb.dma("sync", self.mask_std[:, :], dr["mask_std"][:, :])
        sb.dma("sync", self.mask_lo.ap(),
               dr["mask_lo"].ap().rearrange("(q p) f -> p q f", p=128))
        for l in range(2):
            for gb, name in ((0, "lng"), (1, "lnb")):
                k = l * 2 + gb
                stg = self.stat[0:8, k * 128:(k + 1) * 128]
                sb.dma("sync", stg, dr[name][l, :].rearrange("(p f) -> p f", f=128))
                pdst = self.prow[:, k * 8:k * 8 + 8]
                sb.op("tensor", lambda e, stg=stg, pdst=pdst: e.transpose(
                    pdst, stg, self.identf[0:8, 0:8]),
                    reads=[stg, self.identf[0:8, 0:8]], writes=[pdst])
                dstc = self.lgbc[:, l, gb, :]
                sb.op("vector", lambda e, dstc=dstc, pdst=pdst: e.tensor_copy(dstc, pdst),
                      reads=[pdst], writes=[dstc])
            sb.dma("sync", self.qsr[0:1, l, :], dr["qs"][l:l + 1, :])
            sb.dma("sync", self.ksr[0:1, l, :], dr["ks"][l:l + 1, :])
            for hf in range(2):
                off = hf * HID
                sb.dma("sync", self.bstg2[0:21, l, hf, 0:128],
                       dr["b1"][l, off:off + 2688].rearrange("(p f) -> p f", f=128))
                sb.dma("sync", self.bstg2[21:22, l, hf, 0:42],
                       dr["b1"][l, off + 2688:off + HID].unsqueeze(0))
            sb.dma("sync", self.bstg3[0:8, l, 0:128],
                   dr["b2"][l, :].rearrange("(p f) -> p f", f=128))
        sb.op("vector", lambda e: e.memset(self.browones.ap(), 1.0),
              writes=[self.browones.ap()])
        sb.op("vector", lambda e: e.memset(self.onesrow0[:, :], 1.0),
              writes=[self.onesrow0[:, :]])
        sb.op("vector", lambda e: e.memset(self.onescol[:, :], 1.0),
              writes=[self.onescol[:, :]])
        sb.op("vector", lambda e: e.memset(self.epsc[:, :], EPS),
              writes=[self.epsc[:, :]])
        sb.op("vector", lambda e: e.memset(self.epscol[:, :], EPS),
              writes=[self.epscol[:, :]])
        for l in range(2):
            for hf in range(2):
                src_ap = self.bstg2[0:NJ, l, hf, :]
                pdst = self.prow[:, hf * NJ:hf * NJ + NJ]
                sb.op("tensor", lambda e, src_ap=src_ap, pdst=pdst: e.transpose(
                    pdst, src_ap, self.identf[0:NJ, 0:NJ]),
                    reads=[src_ap, self.identf[0:NJ, 0:NJ]], writes=[pdst])
            sb.op("vector", lambda e, l=l: e.tensor_copy(self.b1c[:, l, :], self.prow[:, 0:2 * NJ]),
                  reads=[self.prow[:, 0:2 * NJ]], writes=[self.b1c[:, l, :]])
            src_ap = self.bstg3[0:NCH, l, :]
            pdst = self.prow[:, 64:64 + NCH]
            sb.op("tensor", lambda e, src_ap=src_ap, pdst=pdst: e.transpose(
                pdst, src_ap, self.identf[0:NCH, 0:NCH]),
                reads=[src_ap, self.identf[0:NCH, 0:NCH]], writes=[pdst])
            sb.op("vector", lambda e, l=l: e.tensor_copy(self.b2c[:, l, :], self.prow[:, 64:64 + NCH]),
                  reads=[self.prow[:, 64:64 + NCH]], writes=[self.b2c[:, l, :]])

    def _weights(self):
        sb, dr, dm = self.sb, self.dr, self.dm
        jobs = []

        def wchunk(sname, dname, l, r0, r1):
            jobs.append(((sname, l), lambda: sb.dma(
                "gpsimd", dm[sname][l][r0:r1, :], dr[dname][l][r0:r1, :])))

        def wdchunk(s):
            jobs.append((("s_Wd", 0), lambda: sb.dma(
                "gpsimd",
                dm["s_Wd"][s * D:(s + 1) * D, :].rearrange("(h d) c -> h d c", h=KH),
                dr["Wd"].ap().rearrange("(h s d) c -> s h d c", h=KH, s=3)[s])))

        for l in range(2):
            for r in range(0, D, 256):
                wchunk("s_w1", "w1", l, r, r + 256)
            for r0 in (0, 910, 1820):
                wchunk("s_w2", "w2", l, r0, min(r0 + 910, HID))
            if l == 0:
                for s in range(3):
                    wdchunk(s)
        self._wjobs = jobs

    def _drain_wjob(self, n=1):
        for _ in range(n):
            if self._wjobs:
                self._wjobs.pop(0)[1]()

    def _drain_wtag(self, tag):
        while any(t == tag for t, _ in self._wjobs):
            self._wjobs.pop(0)[1]()

    def ln_stats(self, src_fn, sqbuf, ci, w=512, layer=None):
        sb = self.sb
        b0 = (ci % 2) * 1024
        pSum = self.pD[:, b0:b0 + w]
        pSq = self.pD[:, b0 + 512:b0 + 512 + w]
        for ch in range(NCH):
            src = src_fn(ch)
            sb.op("scalar", lambda e, src=src: e.activation(sqbuf, src, AF.Square),
                  reads=[src], writes=[sqbuf])
            sb.op("tensor", lambda e, src=src, ch=ch: e.matmul(
                pSum[0:1, :], self.onescol[:, :], src,
                start=(ch == 0), stop=(ch == NCH - 1)),
                reads=[self.onescol[:, :], src], writes=[pSum[0:1, :]])
            sb.op("tensor", lambda e, ch=ch: e.matmul(
                pSq[0:1, :], self.onescol[:, :], sqbuf,
                start=(ch == 0), stop=(ch == NCH - 1)),
                reads=[self.onescol[:, :], sqbuf], writes=[pSq[0:1, :]])
        L = self.rows[0:1, ci % 2]
        bo = self.browones[:, ci % 2, :]
        cs = slice(0, w)
        sb.op("vector", lambda e, L=L, pSum=pSum: e.tensor_scalar(
            L[:, 0, cs], pSum[0:1, :], 1.0 / D, None, ALU.mult),
            reads=[pSum[0:1, :]], writes=[L[:, 0, cs]])
        sb.op("scalar", lambda e, L=L: e.activation(L[:, 2, cs], L[:, 0, cs], AF.Square),
              reads=[L[:, 0, cs]], writes=[L[:, 2, cs]])
        sb.op("vector", lambda e, L=L, pSq=pSq: e.scalar_tensor_tensor(
            L[:, 1, cs], pSq[0:1, :], 1.0 / D, L[:, 2, cs], ALU.mult, ALU.subtract),
            reads=[pSq[0:1, :], L[:, 2, cs]], writes=[L[:, 1, cs]])


    def ln_apply(self, src_fn, dst_fn, ci, w=512, layer=None):
        sb = self.sb
        pa, pb = self.ps4, self.ps5
        L = self.rows[0:1, ci % 2]
        cs = slice(0, w)
        ones = self.onesrow0[0:1, :]
        sb.op("tensor", lambda e: e.matmul(
            pa[:, 0:w], ones, L[:, 1, cs], start=True, stop=True),
            reads=[ones, L[:, 1, cs]], writes=[pa[:, 0:w]])
        sb.op("tensor", lambda e: e.matmul(
            pb[:, 0:w], ones, L[:, 0, cs], start=True, stop=True),
            reads=[ones, L[:, 0, cs]], writes=[pb[:, 0:w]])
        rst = self.stat[:, 1024 + (ci % 2) * 512:1024 + (ci % 2) * 512 + w]
        sb.op("scalar", lambda e, rst=rst: e.activation(
            rst, pa[:, 0:w], AF.Sqrt, bias=self.epscol[:, :]),
            reads=[pa[:, 0:w], self.epscol[:, :]], writes=[rst])
        sb.op("vector", lambda e, rst=rst: e.reciprocal(rst, rst),
              reads=[rst], writes=[rst])
        for ch in range(NCH):
            src = src_fn(ch)
            dst = dst_fn(ch)
            t1 = self.stat[:, (ch % 2) * 512:(ch % 2) * 512 + w]
            sb.op("vector", lambda e, src=src, t1=t1: e.tensor_tensor(
                t1, src, pb[:, 0:w], ALU.subtract),
                reads=[src, pb[:, 0:w]], writes=[t1])
            if layer is None:
                sb.op("vector", lambda e, dst=dst, t1=t1, rst=rst: e.tensor_tensor(
                    dst, t1, rst, ALU.mult),
                    reads=[t1, rst], writes=[dst])
            else:
                sb.op("vector", lambda e, t1=t1, rst=rst: e.tensor_tensor(
                    t1, t1, rst, ALU.mult),
                    reads=[t1, rst], writes=[t1])
                g = self.lgbc[:, layer, 0, ch:ch + 1]
                b = self.lgbc[:, layer, 1, ch:ch + 1]
                sb.op("scalar", lambda e, dst=dst, t1=t1, g=g, b=b: e.activation(
                    dst, t1, AF.Identity, bias=b, scale=g),
                    reads=[t1, g, b], writes=[dst])

    def ln_apply_bat(self, src3, dst3, tb, ci, nch=NCH):
        sb = self.sb
        pa, pb = self.ps4, self.ps5
        L = self.rows[0:1, ci % 2]
        ones = self.onesrow0[0:1, :]
        sb.op("tensor", lambda e: e.matmul(
            pa[:, 0:512], ones, L[:, 1, 0:512], start=True, stop=True),
            reads=[ones, L[:, 1, 0:512]], writes=[pa[:, 0:512]])
        sb.op("tensor", lambda e: e.matmul(
            pb[:, 0:512], ones, L[:, 0, 0:512], start=True, stop=True),
            reads=[ones, L[:, 0, 0:512]], writes=[pb[:, 0:512]])
        rst = self.stat[:, 1024 + (ci % 2) * 512:1536 + (ci % 2) * 512]
        sb.op("scalar", lambda e, rst=rst: e.activation(
            rst, pa[:, 0:512], AF.Sqrt, bias=self.epscol[:, :]),
            reads=[pa[:, 0:512], self.epscol[:, :]], writes=[rst])
        sb.op("vector", lambda e, rst=rst: e.reciprocal(rst, rst),
              reads=[rst], writes=[rst])
        pbB = pb[:, 0:512].unsqueeze(1).broadcast_to([128, nch, 512])
        rstB = rst.unsqueeze(1).broadcast_to([128, nch, 512])
        sb.op("vector", lambda e: e.tensor_tensor(tb, src3, pbB, ALU.subtract),
              reads=[src3, pb[:, 0:512]], writes=[tb])
        sb.op("vector", lambda e: e.tensor_tensor(dst3, tb, rstB, ALU.mult),
              reads=[tb, rst], writes=[dst3])

    def ph0(self):
        nc, sb, dr, dm = self.nc, self.sb, self.dr, self.dm
        self.ar.push()
        x0tm = self.ar.take([128, 2, D], F32)
        cmt = self.ar.take([128, 2, NCH, 4, 128], F32)
        sq = self.ar.take([128, 2, 512], F32)
        tb = self.ar.take([128, NCH, 512], F32)
        pT = self.prow
        pend = None
        for ci in range(4):
            cb = ci % 2
            for tt in range(4):
                t = ci * 4 + tt
                db = t % 2
                tm = x0tm[:, db, :]
                sb.dma("sync", tm, dr["x0"][t * 128:(t + 1) * 128, :])
                if tt == 0:
                    self._drain_wjob(1)
                for ch in range(NCH):
                    src = x0tm[:, db, ch * 128:(ch + 1) * 128]
                    sb.op("tensor",
                          lambda e, src=src: e.transpose(pT[:, 0:128], src, self.identf[:, :]),
                          reads=[src, self.identf[:, :]], writes=[pT[:, 0:128]])
                    dst = cmt[:, cb, ch, tt, :]
                    sb.op("vector", lambda e, dst=dst: e.tensor_copy(dst, pT[:, 0:128]),
                          reads=[pT[:, 0:128]], writes=[dst])
                    sb.dma("scalar",
                           dm["x0cm"][ch * 128:(ch + 1) * 128, t * 128:(t + 1) * 128], dst)
            self.ln_stats(lambda ch, cb=cb: cmt[:, cb, ch, :, :].rearrange("p a b -> p (a b)"),
                          sq[:, cb, :], ci)
            if pend is not None:
                self.ln_apply_bat(*pend)
            pend = (cmt[:, cb].rearrange("p c a b -> p c (a b)"),
                    self.xt[:, :, 1 + ci * 512:1 + ci * 512 + 512], tb, ci)
        self.ln_apply_bat(*pend)
        self.ar.pop()

    def ffn_layer(self, inv, l, emit_final=False):
        self.cc1()
        self.qkv_attn(l)
        self.wo_residual(l)
        self.cc2()
        self.ln_glu(l, emit_final)

    def cc1(self):
        sb, dm = self.sb, self.dm
        self._drain_wjob(1)
        self.ar.push()
        stg = self.ar.take([128, HALOT], BF16)
        for ch in range(NCH):
            src = self.xt[:, ch, 1 + T - HALOT:1 + T]
            sb.op("scalar", lambda e, src=src, stg=stg: e.copy(stg, src),
                  reads=[src], writes=[stg])
            sb.dma("scalar", dm["cc1_in"][ch * 128:(ch + 1) * 128, :], stg)
        sb.op("gpsimd", lambda e: e.collective_compute(
            "AllGather", ALU.bypass,
            replica_groups=[[0, 1], [2, 3], [4, 5], [6, 7]],
            ins=[dm["cc1_in"].ap()], outs=[dm["cc1_out"].ap()]),
            reads=[dm["cc1_in"].ap()], writes=[dm["cc1_out"].ap()], kind="cc")
        for ch in range(NCH):
            sb.dma("gpsimd", self.xthalo[:, ch, :],
                   dm["cc1_out"][ch * 128:(ch + 1) * 128, :])
        for ch in range(NCH):
            src = self.xthalo[:, ch, HALOT - 1:HALOT]
            dst = self.xt[:, ch, 0:1]
            sb.op("vector", lambda e, src=src, dst=dst: e.tensor_tensor(
                dst, src, self.flag[:, :], ALU.mult),
                reads=[src, self.flag[:, :]], writes=[dst])
        self.ar.pop()

    def qkv_attn(self, l):
        nc, sb, dr, dm = self.nc, self.sb, self.dr, self.dm
        self.ar.push()
        h_bf = self.ar.take([128, NCH, T], BF16)
        wslab = self.ar.take([128, 1, NCH, 3, 128], BF16)
        q_cm = self.ar.take([128, T], BF16)
        k_cm = self.ar.take([128, HALO + T], BF16)
        v_tm = self.ar.take([128, 4 + NT, 2, 65], BF16)
        qk4 = self.ar.take([128, 2, 4, 2, 128], F32)
        t24 = self.ar.take([128, 8, 2, DH], F32)
        t34 = self.stat[:, 512:1536].rearrange("p (a h d) -> p a h d", a=8, h=2)
        tabs = self.ar.take([128, 2, 8, 2, DH], BF16)
        qb16 = self.ar.take([128, 2, 8, 2, DH], BF16)
        qsbig = self.ar.take([128, 2, 8, DH], BF16)
        rn4 = self.ar.take([128, 16], F32)
        oacc = self.ar.take([128, 2, 4, 128], F32)
        obuf = self.ar.take([128, 2, 512], BF16)
        sq4 = self.stat[:, 512:1536]

        for (row, bidx) in ((self.qsr, 0), (self.ksr, 2)):
            src = row[0:1, l, :]
            pd = self.prow[:, 0:DH]
            sb.op("tensor", lambda e, src=src, pd=pd: e.matmul(
                pd, self.onesrow0[0:1, :], src, start=True, stop=True),
                reads=[self.onesrow0[0:1, :], src], writes=[pd])
            d0 = self.qsb[:, bidx, :]
            sb.op("vector", lambda e, d0=d0, pd=pd: e.tensor_copy(d0, pd),
                  reads=[pd], writes=[d0])
            d1 = self.qsb[:, bidx + 1, :]
            sb.op("vector", lambda e, d1=d1, d0=d0: e.tensor_copy(d1[:, 0:32], d0[:, 32:64]),
                  reads=[d0[:, 32:64]], writes=[d1[:, 0:32]])
            sb.op("vector", lambda e, d1=d1, d0=d0: e.tensor_copy(d1[:, 32:64], d0[:, 0:32]),
                  reads=[d0[:, 0:32]], writes=[d1[:, 32:64]])
        for v in range(2):
            for s in range(2):
                srcq = self.qsb[:, s * 2 + v, :].unsqueeze(1).broadcast_to([128, 4, DH])
                dstq = qsbig[:, v].rearrange("p (t s) d -> p t s d", s=2)[:, :, s, :]
                sb.op("vector", lambda e, srcq=srcq, dstq=dstq: e.tensor_copy(dstq, srcq),
                      reads=[self.qsb.ap()], writes=[dstq])
        for ch in range(NCH):
            shift = 0 if ch < 4 else 1
            src = self.xt[:, ch, 1 - shift:1 - shift + T]
            dst = h_bf[:, ch, :]
            sb.op("scalar", lambda e, src=src, dst=dst: e.copy(dst, src),
                  reads=[src], writes=[dst])

        def load_wslab(hp):
            for s in range(3):
                colofs = s * 512 + hp * 128
                sb.dma("gpsimd", wslab[:, 0, :, s, :],
                       dr["wqkv"][l][:, colofs:colofs + 128]
                       .rearrange("(c p) f -> p c f", p=128))

        load_wslab(0)
        for hp in range(HEADS // 2):
            self._drain_wjob(1)
            pend_tail = []
            for gi, g0 in enumerate(list(range(0, NT, 4)) + [-4]):
                halo = g0 < 0
                gp = gi % 2
                trow = (g0 + 4) * 128
                for s in (0, 1):
                    tc = dr[("cq", "ck")[s]][trow:trow + 512, :]
                    ts_ = dr[("sq", "sk")[s]][trow:trow + 512, :]
                    tdst = tabs[:, gp].rearrange("p (t s) c d -> p t s c d", s=2)
                    sb.dma("gpsimd", tdst[:, :, s, 0, :],
                           tc.rearrange("(a p) f -> p a f", p=128))
                    sb.dma("gpsimd", tdst[:, :, s, 1, :],
                           ts_.rearrange("(a p) f -> p a f", p=128))
                for ti in range(4):
                    t = g0 + ti
                    pqkv = self.pD[:, 0:384] if ti % 2 == 0 else self.pD[:, 512:896]
                    for ch in range(NCH):
                        if halo:
                            shift = 0 if ch < 4 else 1
                            cb = 1 - shift + (t + 4) * 128
                            hh = self.xthalo[:, ch, cb:cb + 128]
                        else:
                            hh = h_bf[:, ch, t * 128:(t + 1) * 128]
                        rhs = wslab[:, 0, ch, :, :].rearrange("p a b -> p (a b)")
                        sb.op("tensor", lambda e, hh=hh, rhs=rhs, pqkv=pqkv, ch=ch: e.matmul(
                            pqkv, hh, rhs, start=(ch == 0), stop=(ch == NCH - 1)),
                            reads=[hh, rhs], writes=[pqkv])
                    for side in range(2):
                        dstq = qk4[:, gp, ti, side, :]
                        srcp = pqkv[:, side * 128:side * 128 + 128]
                        sb.op("scalar", lambda e, dstq=dstq, srcp=srcp: e.copy(dstq, srcp),
                              reads=[srcp], writes=[dstq])
                    vdst = v_tm[:, t + 4, :, 0:64]
                    vsrc = pqkv[:, 256:384].rearrange("p (h d) -> p h d", h=2)
                    sb.op("scalar", lambda e, vdst=vdst, vsrc=vsrc: e.copy(vdst, vsrc),
                          reads=[pqkv[:, 256:384]], writes=[vdst])
                    vones = v_tm[:, t + 4, :, 64:65]
                    sb.op("vector", lambda e, vones=vones: e.memset(vones, 1.0),
                          writes=[vones])
                qkg = qk4[:, gp]
                qall = qkg.rearrange("p a b c -> p (a b c)")
                qh = qkg.rearrange("p a b (h d) -> p (a b) h d", h=2)
                sb.op("scalar", lambda e, qall=qall: e.activation(sq4, qall, AF.Square),
                      reads=[qkg], writes=[sq4])
                sqh = sq4.rearrange("p (g d) -> p g d", d=DH)
                sb.op("vector", lambda e, sqh=sqh: e.reduce_sum(
                    rn4[:, :], sqh, axis=AX.X),
                    reads=[sq4], writes=[rn4[:, :]])
                sb.op("scalar", lambda e: e.activation(rn4[:, :], rn4[:, :], AF.Sqrt),
                      reads=[rn4[:, :]], writes=[rn4[:, :]])
                sb.op("vector", lambda e: e.tensor_scalar(
                    rn4[:, :], rn4[:, :], 1e-12, None, ALU.max),
                    reads=[rn4[:, :]], writes=[rn4[:, :]])
                sb.op("vector", lambda e: e.reciprocal(rn4[:, :], rn4[:, :]),
                      reads=[rn4[:, :]], writes=[rn4[:, :]])
                tt0 = tabs[:, gp, :, 0, :]
                tt1 = tabs[:, gp, :, 1, :]
                sb.op("vector", lambda e, tt0=tt0: e.tensor_tensor(
                    tt0, tt0, qsbig[:, 0], ALU.mult),
                    reads=[tt0, qsbig[:, 0]], writes=[tt0])
                sb.op("vector", lambda e, tt1=tt1: e.tensor_tensor(
                    tt1, tt1, qsbig[:, 1], ALU.mult),
                    reads=[tt1, qsbig[:, 1]], writes=[tt1])
                ctab = tt0.unsqueeze(2).broadcast_to([128, 8, 2, DH])
                stab = tt1.unsqueeze(2).broadcast_to([128, 8, 2, DH])
                sb.op("vector", lambda e, qh=qh, ctab=ctab: e.tensor_tensor(
                    t24, qh, ctab, ALU.mult),
                    reads=[qkg, tabs[:, gp]], writes=[t24])
                sb.op("vector", lambda e, qh=qh, stab=stab: e.tensor_tensor(
                    t34[:, :, :, 0:32], qh[:, :, :, 32:64], stab[:, :, :, 0:32], ALU.mult),
                    reads=[qkg, tabs[:, gp]], writes=[t34])
                sb.op("vector", lambda e, qh=qh, stab=stab: e.tensor_tensor(
                    t34[:, :, :, 32:64], qh[:, :, :, 0:32], stab[:, :, :, 32:64], ALU.mult),
                    reads=[qkg, tabs[:, gp]], writes=[t34])
                sb.op("vector", lambda e: e.tensor_tensor(
                    t24, t24, t34, ALU.add),
                    reads=[t24, t34], writes=[t24])
                rb = rn4.rearrange("p (a h) -> p a h", h=2).unsqueeze(-1) \
                    .broadcast_to([128, 8, 2, DH])
                qbo = qb16[:, gp]
                sb.op("vector", lambda e, rb=rb, qbo=qbo: e.tensor_tensor(
                    qbo, t24, rb, ALU.mult),
                    reads=[t24, rn4], writes=[qbo])
                for fn in pend_tail:
                    fn()
                pend_tail = []
                for side in ([1] if halo else [0, 1]):
                    def tail(side=side, g3=gp, g0=g0):
                        pb0 = side * 512
                        for ti in range(4):
                            srcb = qb16[:, g3, ti * 2 + side].rearrange("p a b -> p (a b)")
                            sb.op("tensor", lambda e, srcb=srcb, ti=ti, pb0=pb0: e.transpose(
                                self.pbf[:, pb0 + ti * 128:pb0 + (ti + 1) * 128], srcb,
                                self.ident[:, :]),
                                reads=[srcb, self.ident[:, :]],
                                writes=[self.pbf[:, pb0 + ti * 128:pb0 + (ti + 1) * 128]])
                        if side == 0:
                            dcol = q_cm[:, g0 * 128:(g0 + 4) * 128]
                        else:
                            dcol = k_cm[:, (g0 + 4) * 128:(g0 + 8) * 128]
                        sb.op("scalar", lambda e, dcol=dcol, pb0=pb0: e.copy(
                            dcol, self.pbf[:, pb0:pb0 + 512]),
                            reads=[self.pbf[:, pb0:pb0 + 512]], writes=[dcol])
                    pend_tail.append(tail)
            if hp < HEADS // 2 - 1:
                load_wslab(hp + 1)
            for fn in pend_tail:
                fn()

            units = [(hh, qc) for hh in range(2)
                     for qc in list(range(4, NT)) + list(range(4))]

            def la_scores(u, hh, qc):
                base = hh * 64
                pS = self.pD[:, 0:1024] if u % 2 == 0 else self.pD[:, 1024:2048]
                lq = q_cm[base:base + 64, qc * 128:(qc + 1) * 128]
                k0 = qc * 128
                kdiag = k_cm[base:base + 64, k0 + 512:k0 + 640]
                kfar = k_cm[base:base + 64, k0:k0 + 384]
                kf3 = k_cm[base:base + 64, k0 + 384:k0 + 512]
                sb.op("tensor", lambda e, lq=lq, kdiag=kdiag, pS=pS: e.matmul(
                    pS[:, 0:128], lq, kdiag, start=True, stop=True),
                    reads=[lq, kdiag], writes=[pS[:, 0:128]])
                sb.op("tensor", lambda e, lq=lq, kfar=kfar, pS=pS: e.matmul(
                    pS[:, 128:512], lq, kfar, start=True, stop=True),
                    reads=[lq, kfar], writes=[pS[:, 128:512]])
                sb.op("tensor", lambda e, lq=lq, kf3=kf3, pS=pS: e.matmul(
                    pS[:, 512:640], lq, kf3, start=True, stop=True),
                    reads=[lq, kf3], writes=[pS[:, 512:640]])

            def la_softmax(u, hh, qc):
                pS = self.pD[:, 0:1024] if u % 2 == 0 else self.pD[:, 1024:2048]
                if qc < 4:
                    m0 = self.mask_lo[:, qc, :]
                    sb.op("vector", lambda e, m0=m0, pS=pS: e.tensor_tensor(
                        pS[:, 0:640], pS[:, 0:640], m0, ALU.add),
                        reads=[pS[:, 0:640], m0], writes=[pS[:, 0:640]])
                else:
                    m0 = self.mask_std[:, 0:256]
                    sb.op("vector", lambda e, m0=m0, pS=pS: e.tensor_tensor(
                        pS[:, 0:256], pS[:, 0:256], m0, ALU.add),
                        reads=[pS[:, 0:256], m0], writes=[pS[:, 0:256]])
                ng = self.negm[:, u % 2:u % 2 + 1]
                sb.op("vector", lambda e, pS=pS, ng=ng: e.reduce_max(
                    ng, pS[:, 0:640], axis=AX.X, negate=True),
                    reads=[pS[:, 0:640]], writes=[ng])
                pq = self.Pq[:, u % 3, :]
                sb.op("scalar", lambda e, pS=pS, ng=ng, pq=pq: e.activation(
                    pq, pS[:, 0:640], AF.Exp, bias=ng),
                    reads=[pS[:, 0:640], ng], writes=[pq])

            def la_trans(u, hh, qc):
                pq = self.Pq[:, u % 3, :]
                for j in range(5):
                    src = pq[:, j * 128:(j + 1) * 128]
                    sb.op("tensor", lambda e, src=src, j=j: e.transpose(
                        self.pbf[:, j * 128:(j + 1) * 128], src, self.ident[:, :]),
                        reads=[src, self.ident[:, :]],
                        writes=[self.pbf[:, j * 128:(j + 1) * 128]])
                ptb = self.PT[:, u % 2, :]
                sb.op("vector", lambda e, ptb=ptb: e.tensor_copy(ptb, self.pbf[:, 0:640]),
                      reads=[self.pbf[:, 0:640]], writes=[ptb])

            def la_av(u, hh, qc):
                pO = self.ps4 if u % 2 == 0 else self.ps5
                tidx = [qc + 4, qc, qc + 1, qc + 2, qc + 3]
                for j in range(5):
                    lv = v_tm[:, tidx[j], hh, :]
                    rp = self.PT[:, u % 2, j * 128:(j + 1) * 128]
                    sb.op("tensor", lambda e, lv=lv, rp=rp, pO=pO, j=j: e.matmul(
                        pO[0:65, 0:128], lv, rp, start=(j == 0), stop=(j == 4)),
                        reads=[lv, rp], writes=[pO[0:65, 0:128]])

            def la_osave(u, hh, qc):
                pO = self.ps4 if u % 2 == 0 else self.ps5
                b = (u // 4) % 2
                k = u % 4
                rr = self.stat[0:1, b * 512 + k * 128:b * 512 + k * 128 + 128]
                sb.op("scalar", lambda e, rr=rr, pO=pO: e.copy(rr, pO[64:65, 0:128]),
                      reads=[pO[64:65, 0:128]], writes=[rr])
                dst = oacc[0:64, b, k, :]
                sb.op("vector", lambda e, dst=dst, pO=pO: e.tensor_copy(dst, pO[0:64, 0:128]),
                      reads=[pO[0:64, 0:128]], writes=[dst])

            def la_flush(u, hh, qc):
                h = hp * 2 + hh
                b = (u // 4) % 2
                qc0 = qc - 3
                rrow = self.stat[0:1, b * 512:b * 512 + 512]
                sb.op("vector", lambda e, rrow=rrow: e.reciprocal(rrow, rrow),
                      reads=[rrow], writes=[rrow])
                pBC = self.prow
                sb.op("tensor", lambda e, rrow=rrow, pBC=pBC: e.matmul(
                    pBC[0:64, 0:512], self.onesrow0[0:1, 0:64], rrow,
                    start=True, stop=True),
                    reads=[self.onesrow0[0:1, 0:64], rrow], writes=[pBC[0:64, 0:512]])
                bcs = self.stat[0:64, 1024 + b * 512:1536 + b * 512]
                sb.op("scalar", lambda e, bcs=bcs, pBC=pBC: e.copy(bcs, pBC[0:64, 0:512]),
                      reads=[pBC[0:64, 0:512]], writes=[bcs])
                ov = oacc[0:64, b].rearrange("p a b -> p (a b)")
                ob = obuf[0:64, b, :]
                sb.op("gpsimd", lambda e, ov=ov, ob=ob, bcs=bcs: e.tensor_tensor(
                    ob, ov, bcs, ALU.mult),
                    reads=[ov, bcs], writes=[ob])
                sb.dma("sync", dm["Olcm"][h * 64:(h + 1) * 64,
                                          qc0 * 128:qc0 * 128 + 512], ob)

            U = len(units)
            for u in range(U + 4):
                if u < U:
                    la_scores(u, *units[u])
                if 0 <= u - 1 < U:
                    la_softmax(u - 1, *units[u - 1])
                if 0 <= u - 2 < U:
                    la_trans(u - 2, *units[u - 2])
                    la_av(u - 2, *units[u - 2])
                if 0 <= u - 3 < U:
                    la_osave(u - 3, *units[u - 3])
                    if (u - 3) % 4 == 3:
                        la_flush(u - 3, *units[u - 3])
        self.ar.pop()

    def wo_residual(self, l):
        sb, dm, dr = self.sb, self.dm, self.dr
        self.ar.push()
        wot = self.ar.take([128, 4, NCH, 128], BF16)
        ob = self.ar.take([128, 2, 4, 512], BF16)
        for ic in range(4):
            sb.dma("gpsimd", wot[:, ic, :, :],
                   dr["wo"][l][ic * 128:(ic + 1) * 128, :].rearrange("p (o f) -> p o f", f=128))
        for ci in range(4):
            c0 = ci * 512
            for ic in range(4):
                sb.dma("gpsimd" if ic % 2 else "sync", ob[:, ci % 2, ic, :],
                       dm["Olcm"][ic * 128:(ic + 1) * 128, c0:c0 + 512])
            for oc in range(NCH):
                po = self.ps4 if oc % 2 == 0 else self.ps5
                for ic in range(4):
                    lw = wot[:, ic, oc, :]
                    rr = ob[:, ci % 2, ic, :]
                    sb.op("tensor", lambda e, lw=lw, rr=rr, po=po, ic=ic: e.matmul(
                        po[:, 0:512], lw, rr, start=(ic == 0), stop=(ic == 3)),
                        reads=[lw, rr], writes=[po[:, 0:512]])
                dst = self.xt[:, oc, 1 + c0:1 + c0 + 512]
                sb.op("vector", lambda e, dst=dst, po=po: e.tensor_tensor(
                    dst, dst, po[:, 0:512], ALU.add),
                    reads=[dst, po[:, 0:512]], writes=[dst])
        self.ar.pop()

    def cc2(self):
        sb, dm = self.sb, self.dm
        stg = self.stat[:, 0:8]
        stg2 = self.stat[:, 8:16]
        for ch in range(NCH):
            src = self.xt[:, ch, T:T + 1]
            dst = stg[:, ch:ch + 1]
            sb.op("scalar", lambda e, src=src, dst=dst: e.copy(dst, src),
                  reads=[src], writes=[dst])
        sb.dma("scalar", dm["cc2_in"].ap(), stg)
        sb.op("gpsimd", lambda e: e.collective_compute(
            "AllGather", ALU.bypass,
            replica_groups=[[0, 1], [2, 3], [4, 5], [6, 7]],
            ins=[dm["cc2_in"].ap()], outs=[dm["cc2_out"].ap()]),
            reads=[dm["cc2_in"].ap()], writes=[dm["cc2_out"].ap()], kind="cc")
        sb.dma("gpsimd", stg2, dm["cc2_out"][0:128, :])
        fb = self.flag[:, 0:1].broadcast_to([128, 8])
        sb.op("vector", lambda e, fb=fb: e.tensor_tensor(stg2, stg2, fb, ALU.mult),
              reads=[stg2, self.flag[:, :]], writes=[stg2])
        for ch in range(NCH):
            src = stg2[:, ch:ch + 1]
            dst = self.xt[:, ch, 0:1]
            sb.op("scalar", lambda e, src=src, dst=dst: e.copy(dst, src),
                  reads=[src], writes=[dst])

    def ln_glu(self, l, emit_final=False):
        sb, dm = self.sb, self.dm
        self._drain_wtag(("s_w1", l))
        self._drain_wtag(("s_w2", l))
        self.ar.push()
        ln_h = self.ar.take([128, NCH, T], BF16)
        glu = self.ar.take([128, NJ, 512], BF16)
        wa = self.ar.take([128, 2, NCH, 128], BF16)
        wg = self.ar.take([128, 2, NCH, 128], BF16)
        w2t = self.ar.take([128, 2, 4, 256], BF16)
        sqb = self.ar.take([128, 2, 512], F32)
        gg = self.ar.take([128, 2, 512], F32)

        def lsrc(ch, c0):
            shift = 0 if ch < 4 else 1
            return self.xt[:, ch, 1 - shift + c0:1 - shift + c0 + 512]

        pend = None
        for ci in range(4):
            c0 = ci * 512
            self.ln_stats(lambda ch, c0=c0: lsrc(ch, c0), sqb[:, ci % 2, :], ci, layer=l)
            if pend is not None:
                self.ln_apply(*pend)
            pend = (lambda ch, c0=c0: lsrc(ch, c0),
                    lambda ch, c0=c0: ln_h[:, ch, c0:c0 + 512], ci, 512, l)
        self.ln_apply(*pend)

        for ci in range(4):
            self._drain_wjob(1)
            c0 = ci * 512
            for j in range(NJ):
                mj = 128 if j < NJ - 1 else HID - 128 * (NJ - 1)
                db = j % 2
                sb.dma("sync", wa[:, db, :, 0:mj],
                       dm["s_w1"][l][:, j * 128:j * 128 + mj]
                       .rearrange("(c p) f -> p c f", p=128))
                sb.dma("gpsimd", wg[:, db, :, 0:mj],
                       dm["s_w1"][l][:, HID + j * 128:HID + j * 128 + mj]
                       .rearrange("(c p) f -> p c f", p=128))
                b0 = db * 1024
                pA = self.pD[:, b0:b0 + 512]
                pG = self.pD[:, b0 + 512:b0 + 1024]
                for ch in range(NCH):
                    lr = ln_h[:, ch, c0:c0 + 512]
                    la = wa[:, db, ch, 0:mj]
                    lg_ = wg[:, db, ch, 0:mj]
                    sb.op("tensor", lambda e, la=la, lr=lr, pA=pA, ch=ch, mj=mj: e.matmul(
                        pA[0:mj, :], la, lr, start=(ch == 0), stop=(ch == NCH - 1)),
                        reads=[la, lr], writes=[pA[0:mj, :]])
                    sb.op("tensor", lambda e, lg_=lg_, lr=lr, pG=pG, ch=ch, mj=mj: e.matmul(
                        pG[0:mj, :], lg_, lr, start=(ch == 0), stop=(ch == NCH - 1)),
                        reads=[lg_, lr], writes=[pG[0:mj, :]])
                bg = self.b1c[0:mj, l, NJ + j:NJ + j + 1]
                ba = self.b1c[0:mj, l, j:j + 1]
                ggj = gg[0:mj, db, :]
                sb.op("scalar", lambda e, pG=pG, bg=bg, mj=mj, ggj=ggj: e.activation(
                    ggj, pG[0:mj, :], AF.Gelu, bias=bg),
                    reads=[pG[0:mj, :], bg], writes=[ggj])
                gj = glu[0:mj, j, :]
                sb.op("vector", lambda e, gj=gj, pA=pA, ba=ba, mj=mj, ggj=ggj:
                      e.scalar_tensor_tensor(
                          gj, pA[0:mj, :], ba, ggj, ALU.add, ALU.mult),
                      reads=[pA[0:mj, :], ba, ggj], writes=[gj])
            for pp in range(4):
                def w2load(qi, pp=pp):
                    j0 = qi * 4
                    db = qi % 2
                    if j0 + 4 <= NJ - 1 or (j0 + 4 == NJ and HID % 128 == 0):
                        sb.dma("scalar", w2t[:, db, :, :],
                               dm["s_w2"][l][j0 * 128:j0 * 128 + 512,
                                             pp * 256:pp * 256 + 256]
                               .rearrange("(j p) f -> p j f", p=128))
                    else:
                        for j in range(j0, NJ):
                            mj = 128 if j < NJ - 1 else HID - 128 * (NJ - 1)
                            sb.dma("scalar", w2t[0:mj, db, j - j0, :],
                                   dm["s_w2"][l][j * 128:j * 128 + mj,
                                                 pp * 256:pp * 256 + 256])
                nq = (NJ + 3) // 4
                w2load(0)
                for qi in range(nq):
                    if qi + 1 < nq:
                        w2load(qi + 1)
                    for j in range(qi * 4, min(qi * 4 + 4, NJ)):
                        mj = 128 if j < NJ - 1 else HID - 128 * (NJ - 1)
                        db = qi % 2
                        gj = glu[0:mj, j, :]
                        for oo in range(2):
                            lw = w2t[0:mj, db, j - qi * 4, oo * 128:(oo + 1) * 128]
                            pm = self.ps4 if oo == 0 else self.ps5
                            sb.op("tensor", lambda e, lw=lw, gj=gj, pm=pm, j=j, mj=mj: e.matmul(
                                pm[:, 0:512], lw, gj, start=(j == 0), stop=(j == NJ - 1)),
                                reads=[lw, gj], writes=[pm[:, 0:512]])
                for oo in range(2):
                    oc = pp * 2 + oo
                    pm = self.ps4 if oo == 0 else self.ps5
                    bb = self.b2c[:, l, oc:oc + 1]
                    dst = self.xt[:, oc, 1 + c0:1 + c0 + 512]
                    sb.op("vector", lambda e, pm=pm, bb=bb, dst=dst: e.scalar_tensor_tensor(
                        dst, pm[:, 0:512], bb, dst, ALU.add, ALU.add),
                        reads=[pm[:, 0:512], bb, dst], writes=[dst])
            if emit_final:
                fxc = gg.rearrange("p a b -> p (a b)").rearrange(
                    "p (c f) -> p c f", f=128)
                self.final_ci(ci, fxc, sqb[:, ci % 2, :])
        self.ar.pop()

    def combine(self, xprev, xnext, scale, osrc=None, y_dst="y2cm"):
        sb, dm = self.sb, self.dm
        self.ar.push()
        xc = self.ar.take([128, NCH, 512], F32)
        x1c8 = self.ar.take([128, 2, NCH, 512], F32)
        y2b = self.ar.take([128, NCH, 512], BF16)
        sqb = self.ar.take([128, 2, 512], F32)
        tb = self.ar.take([128, NCH, 512], F32)
        pend = None
        for ci in range(4):
            c0 = ci * 512
            cb = ci % 2
            for ch in range(NCH):
                sb.dma("sync" if ch % 2 else "scalar", xc[:, ch, :],
                       xprev[ch * 128:(ch + 1) * 128, c0:c0 + 512])
                if osrc is not None:
                    sb.dma("gpsimd", tb[:, ch, :],
                           osrc[ch * 128:(ch + 1) * 128, c0:c0 + 512])
            dst = x1c8[:, cb]
            if osrc is None:
                src2 = self.xt[:, :, 1 + c0:1 + c0 + 512]
                sb.op("vector", lambda e, dst=dst, src2=src2: e.scalar_tensor_tensor(
                    dst, src2, scale, xc[:, :, :], ALU.mult, ALU.add),
                    reads=[src2, xc[:, :, :]], writes=[dst])
            else:
                sb.op("vector", lambda e, dst=dst: e.tensor_tensor(
                    dst, tb[:, :, :], xc[:, :, :], ALU.add),
                    reads=[tb[:, :, :], xc[:, :, :]], writes=[dst])
            for ch in range(NCH):
                sb.dma("scalar" if ch % 2 else "sync",
                       xnext[ch * 128:(ch + 1) * 128, c0:c0 + 512], x1c8[:, cb, ch, :])
            self.ln_stats(lambda ch, cb=cb: x1c8[:, cb, ch, :], sqb[:, cb, :], ci)
            if pend is not None:
                self._combine_apply(pend, y_dst, dm, tb)
            if y_dst == "y2cm":
                pend = (x1c8[:, cb], y2b[:, :, :], ci, c0)
            else:
                pend = (x1c8[:, cb], self.xt[:, :, 1 + c0:1 + c0 + 512], ci, c0)
        self._combine_apply(pend, y_dst, dm, tb)
        self.ar.pop()

    def _combine_apply(self, pend, y_dst, dm, tb):
        sb = self.sb
        src3, dst3, ci, c0 = pend
        self.ln_apply_bat(src3, dst3, tb[:, :, :], ci)
        if y_dst == "y2cm":
            for ch in range(NCH):
                sb.dma("sync" if ch % 2 else "gpsimd",
                       dm["y2cm"][ch * 128:(ch + 1) * 128, c0:c0 + 512],
                       dst3[:, ch, :])

    def cc3(self):
        sb, dm = self.sb, self.dm
        sb.op("gpsimd", lambda e: e.collective_compute(
            "AllGather", ALU.bypass,
            replica_groups=[[0, 1], [2, 3], [4, 5], [6, 7]],
            ins=[dm["y2cm"].ap()], outs=[dm["y2full"].ap()]),
            reads=[dm["y2cm"].ap()], writes=[dm["y2full"].ap()], kind="cc")

    def dense_attn(self):
        sb, dm = self.sb, self.dm
        self._drain_wtag(("s_Wd", 0))
        self.ar.push()
        Wqt = self.ar.take([128, 2, NCH, 128], BF16)
        Wkt = self.ar.take([128, 2, NCH, 128], BF16)
        Wvt = self.ar.take([128, 2, NCH, 128], BF16)
        Q_hp = self.ar.take([128, T], BF16)
        K_hp = self.ar.take([128, 2 * T], BF16)
        V_cm = self.ar.take([128, 2, 512], BF16)
        V_hp = self.ar.take([128, 32, 2, 65], BF16)
        y2c = self.ar.take([128, 2, NCH, 512], BF16)
        PTD = self.ar.take([128, 2, 1024], BF16)
        ost = self.ar.take([128, 2, 512], F32)
        rcp = self.ar.take([128, 2, 1024 + 512], F32)
        def wload(hp):
            wb = hp % 2
            for ch in range(NCH):
                sb.dma("sync", Wqt[:, wb, ch, :],
                       dm["s_Wd"][hp * 128:hp * 128 + 128, ch * 128:(ch + 1) * 128],
                       transpose=True)
                sb.dma("sync", Wkt[:, wb, ch, :],
                       dm["s_Wd"][D + hp * 128:D + hp * 128 + 128, ch * 128:(ch + 1) * 128],
                       transpose=True)
                sb.dma("sync", Wvt[:, wb, ch, :],
                       dm["s_Wd"][2 * D + hp * 128:2 * D + hp * 128 + 128,
                                  ch * 128:(ch + 1) * 128],
                       transpose=True)

        wload(0)
        for hp in range(KH // 2):
            wb = hp % 2
            for cj in range(12):
                db = cj % 2
                if cj < 4:
                    c0 = cj * 512
                    for ch in range(NCH):
                        sb.dma("gpsimd" if ch % 2 else "sync", y2c[:, db, ch, :],
                               dm["y2cm"][ch * 128:(ch + 1) * 128, c0:c0 + 512])
                    wt, dst = Wqt[:, wb], Q_hp[:, c0:c0 + 512]
                else:
                    tok0 = (cj - 4) * 512
                    slot, col = tok0 // T, tok0 % T
                    for ch in range(NCH):
                        sb.dma("gpsimd" if ch % 2 else "sync", y2c[:, db, ch, :],
                               dm["y2full"][slot * D + ch * 128:slot * D + (ch + 1) * 128,
                                            col:col + 512])
                    wt, dst = Wkt[:, wb], K_hp[:, tok0:tok0 + 512]
                pk = self.pD[:, 0:512] if db == 0 else self.pD[:, 512:1024]
                for ch in range(NCH):
                    lw = wt[:, ch, :]
                    rr = y2c[:, db, ch, :]
                    sb.op("tensor", lambda e, lw=lw, rr=rr, pk=pk, ch=ch: e.matmul(
                        pk, lw, rr, start=(ch == 0), stop=(ch == NCH - 1)),
                        reads=[lw, rr], writes=[pk])
                sb.op("scalar", lambda e, dst=dst, pk=pk: e.copy(dst, pk),
                      reads=[pk], writes=[dst])
                if cj >= 4:
                    pv = self.pD[:, 1024:1536] if db == 0 else self.pD[:, 1536:2048]
                    for ch in range(NCH):
                        lw = Wvt[:, wb, ch, :]
                        rr = y2c[:, db, ch, :]
                        sb.op("tensor", lambda e, lw=lw, rr=rr, pv=pv, ch=ch: e.matmul(
                            pv, lw, rr, start=(ch == 0), stop=(ch == NCH - 1)),
                            reads=[lw, rr], writes=[pv])
                    vc = V_cm[:, db, :]
                    sb.op("scalar", lambda e, vc=vc, pv=pv: e.copy(vc, pv),
                          reads=[pv], writes=[vc])
                    for tt in range(4):
                        src = V_cm[:, db, tt * 128:(tt + 1) * 128]
                        sb.op("tensor", lambda e, src=src, tt=tt: e.transpose(
                            self.pbf[:, tt * 128:(tt + 1) * 128], src, self.ident[:, :]),
                            reads=[src, self.ident[:, :]],
                            writes=[self.pbf[:, tt * 128:(tt + 1) * 128]])
                    kc0 = (cj - 4) * 4
                    for tt in range(4):
                        vd = V_hp[:, kc0 + tt, :, 0:64]
                        psrc = self.pbf[:, tt * 128:(tt + 1) * 128] \
                            .rearrange("p (h d) -> p h d", h=2)
                        sb.op("vector", lambda e, vd=vd, psrc=psrc: e.tensor_copy(vd, psrc),
                              reads=[self.pbf[:, tt * 128:(tt + 1) * 128]], writes=[vd])
                        vones = V_hp[:, kc0 + tt, :, 64:65]
                        sb.op("vector", lambda e, vones=vones: e.memset(vones, 1.0),
                              writes=[vones])
            if hp < KH // 2 - 1:
                wload(hp + 1)
            for hh in range(2):
                h = hp * 2 + hh
                base = hh * 64

                def d_scores(p, q0):
                    b0 = (p % 2) * 1024
                    lq = Q_hp[base:base + 64, q0:q0 + 512]
                    for o in range(2):
                        kc = 2 * p + o
                        lk = K_hp[base:base + 64, kc * 128:(kc + 1) * 128]
                        dstp = self.pD[:, b0 + o * 512:b0 + o * 512 + 512]
                        sb.op("tensor", lambda e, lk=lk, lq=lq, dstp=dstp: e.matmul(
                            dstp, lk, lq, start=True, stop=True),
                            reads=[lk, lq], writes=[dstp])

                def d_expav(p, q0, pO):
                    b0 = (p % 2) * 1024
                    pt = PTD[:, p % 2, :]
                    sb.op("scalar", lambda e, pt=pt, b0=b0: e.activation(
                        pt, self.pD[:, b0:b0 + 1024], AF.Exp, scale=0.125),
                        reads=[self.pD[:, b0:b0 + 1024]], writes=[pt])
                    for o in range(2):
                        kc = 2 * p + o
                        lv = V_hp[:, kc, hh, :]
                        rp = PTD[:, p % 2, o * 512:o * 512 + 512]
                        sb.op("tensor", lambda e, lv=lv, rp=rp, pO=pO, kc=kc: e.matmul(
                            pO[0:65, 0:512], lv, rp, start=(kc == 0), stop=(kc == 31)),
                            reads=[lv, rp], writes=[pO[0:65, 0:512]])

                for qb in range(4):
                    q0 = qb * 512
                    pO = self.ps4 if qb % 2 == 0 else self.ps5
                    prevp = None
                    for p in range(16):
                        d_scores(p, q0)
                        if prevp is not None:
                            d_expav(prevp, q0, pO)
                        prevp = p
                    d_expav(prevp, q0, pO)
                    rr = rcp[0:1, qb % 2, 0:512]
                    sb.op("scalar", lambda e, rr=rr, pO=pO: e.copy(rr, pO[64:65, 0:512]),
                          reads=[pO[64:65, 0:512]], writes=[rr])
                    osv = rcp[0:64, qb % 2, 1024:1536]
                    sb.op("scalar", lambda e, osv=osv, pO=pO: e.copy(osv, pO[0:64, 0:512]),
                          reads=[pO[0:64, 0:512]], writes=[osv])
                    sb.op("vector", lambda e, rr=rr: e.reciprocal(rr, rr),
                          reads=[rr], writes=[rr])
                    pBC = self.prow
                    sb.op("tensor", lambda e, rr=rr, pBC=pBC: e.matmul(
                        pBC[0:64, 0:512], self.onesrow0[0:1, 0:64], rr,
                        start=True, stop=True),
                        reads=[self.onesrow0[0:1, 0:64], rr], writes=[pBC[0:64, 0:512]])
                    bcs = rcp[0:64, qb % 2, 512:1024]
                    sb.op("scalar", lambda e, bcs=bcs, pBC=pBC: e.copy(bcs, pBC[0:64, 0:512]),
                          reads=[pBC[0:64, 0:512]], writes=[bcs])
                    oso = ost[0:64, qb % 2, :]
                    sb.op("gpsimd", lambda e, bcs=bcs, osv=osv, oso=oso: e.tensor_tensor(
                        oso, osv, bcs, ALU.mult),
                        reads=[osv, bcs], writes=[oso])
                    sb.dma("sync", dm["Ocm"][h * 64:(h + 1) * 64, q0:q0 + 512], oso)
        self.ar.pop()

    def final_out(self):
        sb, dm = self.sb, self.dm
        self.ar.push()
        xc = self.ar.take([128, 2, 512], F32)
        x3c8 = self.ar.take([128, NCH, 512], F32)
        stg = self.ar.take([128, 2, D], F32)
        pT = self.prow
        for ci in range(4):
            c0 = ci * 512
            for ch in range(NCH):
                db = ch % 2
                sb.dma("sync", xc[:, db, :], dm["x2cm"][ch * 128:(ch + 1) * 128, c0:c0 + 512])
                dst = x3c8[:, ch, :]
                src2 = self.xt[:, ch, 1 + c0:1 + c0 + 512]
                sb.op("vector", lambda e, dst=dst, src2=src2, db=db: e.scalar_tensor_tensor(
                    dst, src2, 0.5, xc[:, db, :], ALU.mult, ALU.add),
                    reads=[src2, xc[:, db, :]], writes=[dst])
            for tt in range(4):
                t = ci * 4 + tt
                sg = stg[:, tt % 2, :]
                for ch in range(NCH):
                    src = x3c8[:, ch, tt * 128:(tt + 1) * 128]
                    sb.op("tensor", lambda e, src=src: e.transpose(
                        pT[:, 0:128], src, self.identf[:, :]),
                        reads=[src, self.identf[:, :]], writes=[pT[:, 0:128]])
                    dstc = sg[:, ch * 128:(ch + 1) * 128]
                    sb.op("vector", lambda e, dstc=dstc: e.tensor_copy(dstc, pT[:, 0:128]),
                          reads=[pT[:, 0:128]], writes=[dstc])
                sb.dma("sync", self.out[t * 128:(t + 1) * 128, :], sg[:, 0:D])
        self.ar.pop()

    def final_ci(self, ci, fxc, fstg):
        sb, dm = self.sb, self.dm
        pT = self.prow
        for tt in range(4):
            t = ci * 4 + tt
            for ch in range(NCH):
                sb.dma("sync" if ch % 2 else "scalar", fxc[:, ch, :],
                       dm["x2cm"][ch * 128:(ch + 1) * 128, t * 128:(t + 1) * 128])
            src2 = self.xt[:, :, 1 + t * 128:1 + (t + 1) * 128]
            sb.op("vector", lambda e, src2=src2: e.scalar_tensor_tensor(
                fxc, src2, 0.5, fxc, ALU.mult, ALU.add),
                reads=[src2, fxc], writes=[fxc])
            for half in range(2):
                for k in range(4):
                    srcb = fxc[:, half * 4 + k, :]
                    sb.op("tensor", lambda e, srcb=srcb, k=k: e.transpose(
                        pT[:, k * 128:(k + 1) * 128], srcb, self.identf[:, :]),
                        reads=[srcb, self.identf[:, :]],
                        writes=[pT[:, k * 128:(k + 1) * 128]])
                dsth = fstg[:, 0:512]
                sb.op("vector", lambda e, dsth=dsth: e.tensor_copy(dsth, pT[:, 0:512]),
                      reads=[pT[:, 0:512]], writes=[dsth])
                sb.dma("sync", self.out[t * 128:(t + 1) * 128,
                                        half * 512:(half + 1) * 512], dsth)

    def _emit_out_from_xt(self):
        sb = self.sb
        pT = self.prow
        stg = self.stat
        for t in range(NT):
            for ch in range(NCH):
                src = self.xt[:, ch, 1 + t * 128:1 + (t + 1) * 128]
                sb.op("tensor",
                      lambda e, src=src: e.transpose(pT[:, 0:128], src, self.identf[:, :]),
                      reads=[src, self.identf[:, :]], writes=[pT[:, 0:128]])
                dst = stg[:, ch * 128:(ch + 1) * 128]
                sb.op("vector", lambda e, dst=dst: e.tensor_copy(dst, pT[:, 0:128]),
                      reads=[pT[:, 0:128]], writes=[dst])
            sb.dma("sync", self.out[t * 128:(t + 1) * 128, :], stg[:, 0:D])



def _in_maps(L0_attn_z, L0_attn_W_qkv, L0_ffn_wqkv, L0_ffn_q_scale, L0_ffn_k_scale,
             L0_ffn_wo, L0_ffn_ln_g, L0_ffn_ln_b, L0_ffn_w1, L0_ffn_b1, L0_ffn_w2,
             L0_ffn_b2):
    maps = []
    for c in range(NCORES):
        bat, half = c // 2, c % 2
        m = {
            "x0": np.ascontiguousarray(np.asarray(L0_attn_z)[bat, half * T:(half + 1) * T, :],
                                       dtype=np.float32),
            "Wd": np.ascontiguousarray(L0_attn_W_qkv, dtype=np.float32),
            "wqkv": np.ascontiguousarray(L0_ffn_wqkv, dtype=np.float32),
            "qs": np.ascontiguousarray(L0_ffn_q_scale, dtype=np.float32),
            "ks": np.ascontiguousarray(L0_ffn_k_scale, dtype=np.float32),
            "wo": np.ascontiguousarray(L0_ffn_wo, dtype=np.float32),
            "lng": np.ascontiguousarray(L0_ffn_ln_g, dtype=np.float32),
            "lnb": np.ascontiguousarray(L0_ffn_ln_b, dtype=np.float32),
            "w1": np.ascontiguousarray(L0_ffn_w1, dtype=np.float32),
            "b1": np.ascontiguousarray(L0_ffn_b1, dtype=np.float32),
            "w2": np.ascontiguousarray(L0_ffn_w2, dtype=np.float32),
            "b2": np.ascontiguousarray(L0_ffn_b2, dtype=np.float32),
        }
        m.update(_host_tables(half))
        maps.append(m)
    return maps


def kernel(**inputs):
    b = Build().build()
    maps = _in_maps(**inputs)
    res = run_bass_kernel_spmd(b.nc, maps, core_ids=list(range(NCORES)))
    full = np.zeros((B, N, D), np.float32)
    for c in range(NCORES):
        bat, half = c // 2, c % 2
        full[bat, half * T:(half + 1) * T, :] = np.asarray(res.results[c]["out"])
    return full


# version 24 (final)
# speedup vs baseline: 1.0222x; 1.0222x over previous
import contextlib
import numpy as np
import ml_dtypes
import concourse.bass as bass
import concourse.mybir as mybir
from concourse.bass_utils import run_bass_kernel_spmd

F32 = mybir.dt.float32
BF16 = mybir.dt.bfloat16
U8 = mybir.dt.uint8
AF = mybir.ActivationFunctionType
ALU = mybir.AluOpType
AX = mybir.AxisListType

B, N, D = 4, 4096, 1024
NCORES = 8
T = 2048
NT = T // 128
NCH = D // 128
HEADS, DH, WINDOW = 8, 64, 512
HID, HID2 = 2730, 5460
NJ = 22
QK_SCALE = 8.0
KH = 16
HALO = 512
HALOT = HALO + 1
EPS = 1e-5
BIG = 1e30

TAPS = []
BUILD_UPTO = "full"



def _rotary_tables(pos):
    half = np.arange(0, DH, 2, dtype=np.float64)
    inv_freq = 1.0 / 10000.0 ** (half / DH)
    freqs = pos[:, None].astype(np.float64) * inv_freq[None, :]
    freqs = np.concatenate([freqs, freqs], -1)
    base = (half + 0.4 * DH) / (1.4 * DH)
    power = (pos.astype(np.float64) - N // 2) / float(WINDOW // 2)
    scale = base[None, :] ** power[:, None]
    scale = np.concatenate([scale, scale], -1)
    return np.cos(freqs), np.sin(freqs), scale


def _host_tables(half):
    own0 = half * T
    pos = np.concatenate([np.arange(own0 - HALO, own0), np.arange(own0, own0 + T)])
    pos = np.maximum(pos, 0)
    cos, sin, scale = _rotary_tables(pos)
    sgn = np.concatenate([-np.ones(32), np.ones(32)])[None, :]
    cq = (cos * scale * QK_SCALE).astype(np.float32)
    sq = (sgn * sin * scale * QK_SCALE).astype(np.float32)
    ck = (cos / scale).astype(np.float32)
    sk = (sgn * sin / scale).astype(np.float32)

    def chunk_mask(qc, kc):
        m = np.full((128, 128), -1e9, np.float32)
        qpos = own0 + qc * 128 + np.arange(128)[:, None]
        kpos = own0 + kc * 128 + np.arange(128)[None, :]
        ok = (qpos - kpos >= 0) & (qpos - kpos < WINDOW) & (kpos >= 0)
        m[ok] = 0.0
        return m

    mask_std = np.concatenate([chunk_mask(8, 8), chunk_mask(8, 4),
                               np.zeros((128, 384), np.float32)], axis=1)
    mask_lo = np.zeros((4, 128, 640), np.float32)
    for qc in range(4):
        chunks = [qc, qc - 4, qc - 3, qc - 2, qc - 1]
        mask_lo[qc] = np.concatenate([chunk_mask(qc, kc) for kc in chunks], axis=1)

    return {
        "cq": cq, "sq": sq, "ck": ck, "sk": sk,
        "mask_std": mask_std, "mask_lo": mask_lo.reshape(4 * 128, 640),
        "flagcol": np.full((128, 1), float(half), np.float32),
        "ident": np.eye(128, dtype=ml_dtypes.bfloat16),
        "identf": np.eye(128, dtype=np.float32),
    }



ENGINES = ("sync", "scalar", "vector", "tensor", "gpsimd")
_PSUM_BANK = 2048
_DT_SIZE = {F32: 4, BF16: 2, U8: 1, mybir.dt.uint32: 4, mybir.dt.float16: 2}


class SB:

    def __init__(self, nc, stack):
        self.nc = nc
        self.eng = {"sync": nc.sync, "scalar": nc.scalar, "vector": nc.vector,
                    "tensor": nc.tensor, "gpsimd": nc.gpsimd}
        self.sem = {}
        self.cnt = {}
        NDMASEM = 5
        self.dpool = {}
        self.didx = {}
        for e in ENGINES:
            name = "c_" + e
            self.sem[name] = stack.enter_context(nc.semaphore(name))
            self.cnt[name] = 0
            self.dpool[e] = []
            self.didx[e] = 0
            for i in range(NDMASEM):
                name = f"d_{e}{i}"
                self.sem[name] = stack.enter_context(nc.semaphore(name))
                self.cnt[name] = 0
                self.dpool[e].append(name)
        self.sem["cc"] = stack.enter_context(nc.semaphore("cc"))
        self.cnt["cc"] = 0
        self.clock = {e: {} for e in ENGINES}
        self.snap = {}
        self.acc = {}
        self.bank = {}
        self.psum_base = {}
        self.nwaits = 0
        self.nops = 0

    def _boxes(self, ap):
        th = ap.tensor
        name = th.name
        space = str(ap.space)
        esz = _DT_SIZE.get(ap.dtype, 4)
        dims = [list(d) for d in ap.ap]
        off = int(ap.offset)
        if space.endswith("DRAM"):
            if len(dims) == 2 and dims[1][0] == 1 and dims[0][0] >= dims[1][1]:
                rowlen = dims[0][0]
                p0 = off // rowlen
                c0 = off % rowlen
                return (name, p0, p0 + dims[0][1], c0 * esz,
                        (c0 + dims[1][1]) * esz, "D")
            return (name, 0, 1 << 30, 0, 1 << 60, "D")
        sp = "P" if space.endswith("PSUM") else "S"
        step0 = dims[0][0] if dims else 1
        if step0 <= 0 or len(dims) < 1:
            return (name, 0, 128, 0, 1 << 60, sp)
        p0 = off // step0
        c0 = off - p0 * step0
        span = 1
        for st, cntd in dims[1:]:
            span += abs(st) * (cntd - 1)
        if span > step0:
            return (name, p0, p0 + dims[0][1], 0, 1 << 60, sp)
        return (name, p0, p0 + dims[0][1], c0 * esz, (c0 + span) * esz, sp)

    @staticmethod
    def _overlap(a, b):
        return (a[1] < b[2] and b[1] < a[2] and a[3] < b[4] and b[3] < a[4])

    @staticmethod
    def _covers(a, b):
        return (a[1] <= b[1] and a[2] >= b[2] and a[3] <= b[3] and a[4] >= b[4])

    def _need(self, ename, deps, token):
        sem, val = token
        if self.clock[ename].get(sem, 0) >= val:
            return
        deps[sem] = max(deps.get(sem, 0), val)

    def _collect(self, ename, rboxes, wboxes):
        deps = {}
        for box in rboxes:
            for kind, b2, tok, eng2 in reversed(self.acc.get(box[0], ())):
                if kind == "w" and self._overlap(box, b2):
                    if not (eng2 == ename == "tensor"):
                        self._need(ename, deps, tok)
                    if self._covers(b2, box):
                        break
        for box in wboxes:
            for kind, b2, tok, eng2 in reversed(self.acc.get(box[0], ())):
                if self._overlap(box, b2):
                    if not (eng2 == ename == "tensor"):
                        self._need(ename, deps, tok)
                    if kind == "w" and self._covers(b2, box):
                        break
        return deps

    def _banks(self, box):
        base = self.psum_base.get(box[0])
        if base is None:
            try:
                base = int(self.nc.lookup_mloc(box[0]).bank)
            except Exception:
                base = 0
            self.psum_base[box[0]] = base
        b1 = max(box[3], box[4] - 1)
        return range(base + box[3] // _PSUM_BANK, base + b1 // _PSUM_BANK + 1)

    def op(self, ename, emit, reads=(), writes=(), kind="c"):
        eng = self.eng[ename]
        rb = [self._boxes(a) for a in reads]
        wb = [self._boxes(a) for a in writes]
        deps = self._collect(ename, rb, wb)
        for box in rb + wb:
            if box[5] == "P":
                for bk in self._banks(box):
                    last = self.bank.get(bk)
                    if last is not None and last[0] != ename:
                        self._need(ename, deps, last[1])
        if kind == "d":
            sname = self.dpool[ename][self.didx[ename] % len(self.dpool[ename])]
            self.didx[ename] += 1
            if self.cnt[sname]:
                self._need(ename, deps, (sname, self.cnt[sname]))
        elif kind == "cc":
            sname = "cc"
            if self.cnt[sname]:
                self._need(ename, deps, (sname, self.cnt[sname]))
        for sem, val in sorted(deps.items()):
            eng.wait_ge(self.sem[sem], val)
            self.nwaits += 1
            cl = self.clock[ename]
            cl[sem] = max(cl.get(sem, 0), val)
            for s2, v2 in self.snap.get((sem, val), {}).items():
                cl[s2] = max(cl.get(s2, 0), v2)
        inst = emit(eng)
        self.nops += 1
        if kind == "d":
            step = 16
        elif kind == "cc":
            step = 1
        else:
            sname, step = "c_" + ename, 1
        self.cnt[sname] += step
        val = self.cnt[sname]
        inst.then_inc(self.sem[sname], step)
        token = (sname, val)
        snapshot = dict(self.clock[ename])
        snapshot[sname] = val
        self.snap[token] = snapshot
        for box in rb:
            self.acc.setdefault(box[0], []).append(("r", box, token, ename))
        for box in wb:
            self.acc.setdefault(box[0], []).append(("w", box, token, ename))
        for box in rb + wb:
            if box[5] == "P":
                for bk in self._banks(box):
                    self.bank[bk] = (ename, token)
        return token

    def dma(self, ename, out, in_, **kw):
        return self.op(ename, lambda e: e.dma_start(out=out, in_=in_, **kw),
                       reads=[in_], writes=[out], kind="d")

    def drain_all(self):
        for sname, val in self.cnt.items():
            if val > 0:
                self.eng["sync"].wait_ge(self.sem[sname], val)



class Arena:
    def __init__(self, tensor, nbytes):
        self.t = tensor
        self.nbytes = nbytes
        self.off = 0
        self.hi = 0
        self.marks = []

    def push(self):
        self.marks.append(self.off)

    def pop(self):
        self.off = self.marks.pop()

    def take(self, shape, dt):
        esz = _DT_SIZE[dt]
        n = int(np.prod(shape[1:])) * esz
        off = (self.off + 127) // 128 * 128
        assert off + n <= self.nbytes, f"arena overflow: {off}+{n} > {self.nbytes}"
        self.off = off + n
        self.hi = max(self.hi, self.off)
        v = self.t[:, off:off + n].bitcast(dt)
        if len(shape) == 2:
            return v
        kw = {chr(ord('a') + i): shape[i + 1] for i in range(len(shape) - 2)}
        pat = "p (" + " ".join(chr(ord('a') + i) for i in range(len(shape) - 1)) + ") -> p " + \
              " ".join(chr(ord('a') + i) for i in range(len(shape) - 1))
        return v.rearrange(pat, **kw)



class Build:
    def __init__(self):
        self.stack = contextlib.ExitStack()
        self.nc = bass.Bass("TRN2", target_bir_lowering=False)
        self.sb = SB(self.nc, self.stack)
        self.tap_t = {}

    def sbuf(self, name, shape, dt):
        return self.stack.enter_context(self.nc.sbuf_tensor(name, shape, dt))

    def psumt(self, name, shape, dt=F32):
        return self.stack.enter_context(self.nc.psum_tensor(name, shape, dt))

    def tap_sbuf(self, name, ap):
        if name not in TAPS:
            return
        if callable(ap):
            ap = ap()
        t = self.nc.dram_tensor("tap_" + name, list(ap.shape), ap.dtype,
                                kind="ExternalOutput")
        self.sb.dma("sync", t.ap(), ap)
        self.tap_t[name] = t

    def build(self):
        nc = self.nc
        dr = {}
        def inp(name, shape, dt=F32):
            dr[name] = nc.dram_tensor(name, shape, dt, kind="ExternalInput")
            return dr[name]
        inp("x0", [T, D])
        inp("Wd", [3072, D])
        inp("wqkv", [2, D, 1536])
        inp("qs", [2, DH])
        inp("ks", [2, DH])
        inp("wo", [2, 512, D])
        inp("lng", [2, D])
        inp("lnb", [2, D])
        inp("w1", [2, D, HID2])
        inp("b1", [2, HID2])
        inp("w2", [2, HID, D])
        inp("b2", [2, D])
        inp("cq", [HALO + T, DH])
        inp("sq", [HALO + T, DH])
        inp("ck", [HALO + T, DH])
        inp("sk", [HALO + T, DH])
        inp("mask_std", [128, 640])
        inp("mask_lo", [4 * 128, 640])
        inp("flagcol", [128, 1])
        inp("ident", [128, 128], BF16)
        inp("identf", [128, 128], F32)
        self.dr = dr
        self.out = nc.dram_tensor("out", [T, D], F32, kind="ExternalOutput")

        dm = {}
        dm["x0cm"] = nc.dram_tensor("x0cm", [D, T], F32)
        dm["x1cm"] = nc.dram_tensor("x1cm", [D, T], F32)
        dm["x2cm"] = nc.dram_tensor("x2cm", [D, T], F32)
        dm["Ocm"] = nc.dram_tensor("Ocm", [D, T], F32)
        dm["Olcm"] = nc.dram_tensor("Olcm", [512, T], BF16)
        dm["s_w1"] = nc.dram_tensor("s_w1", [2, D, HID2], BF16)
        dm["s_w2"] = nc.dram_tensor("s_w2", [2, HID, D], BF16)
        dm["s_Wd"] = nc.dram_tensor("s_Wd", [3072, D], BF16)
        dm["cc1_in"] = nc.dram_tensor("cc1_in", [D, HALOT], BF16)
        dm["cc1_out"] = nc.dram_tensor("cc1_out", [2 * D, HALOT], BF16)
        dm["cc2_in"] = nc.dram_tensor("cc2_in", [128, NCH], F32)
        dm["cc2_out"] = nc.dram_tensor("cc2_out", [256, NCH], F32)
        for k in range(4):
            dm[f"y2c{k}"] = nc.dram_tensor(f"y2c{k}", [D, 512], BF16)
            dm[f"y2f{k}"] = nc.dram_tensor(f"y2f{k}", [2 * D, 512], BF16)
        self.dm = dm

        self.xt = self.sbuf("xt", [128, NCH, 1 + T], F32)
        self.xthalo = self.sbuf("xthalo", [128, NCH, HALOT], BF16)
        self.rows = self.sbuf("rows", [1, 2, 4, 512], F32)
        self.browones = self.sbuf("browones", [2, 2, 512], F32)
        self.onesrow0 = self.sbuf("onesrow0", [1, 128], F32)
        self.mask_std = self.sbuf("mask_std_sb", [128, 640], F32)
        self.mask_lo = self.sbuf("mask_lo_sb", [128, 4, 640], F32)
        self.flag = self.sbuf("flag", [128, 1], F32)
        self.ident = self.sbuf("identsb", [128, 128], BF16)
        self.identf = self.sbuf("identfsb", [128, 128], F32)
        self.onescol = self.sbuf("onescol", [128, 1], F32)
        self.epsc = self.sbuf("epsc", [1, 1], F32)
        self.epscol = self.sbuf("epscol", [128, 1], F32)
        self.lgbc = self.sbuf("lgbc", [128, 2, 2, NCH], F32)
        self.b1c = self.sbuf("b1c", [128, 2, 2 * NJ], F32)
        self.b2c = self.sbuf("b2c", [128, 2, NCH], F32)
        self.bstg2 = self.sbuf("bstg2", [NJ, 2, 2, 128], F32)
        self.bstg3 = self.sbuf("bstg3", [NCH, 2, 128], F32)
        self.qsr = self.sbuf("qsr", [1, 2, DH], F32)
        self.ksr = self.sbuf("ksr", [1, 2, DH], F32)
        self.qsb = self.sbuf("qsb", [128, 4, DH], F32)
        self.negm = self.sbuf("negm", [128, 2], F32)
        self.stat = self.sbuf("stat", [128, 2048], F32)
        self.Pq = self.sbuf("Pq", [128, 3, 640], BF16)
        self.PT = self.sbuf("PT", [128, 2, 640], BF16)
        art = self.sbuf("arena", [128, 83968], U8)
        self.ar = Arena(art, 83968)

        self.pD = self.psumt("pD", [128, 2048], F32)
        self.ps4 = self.psumt("ps4", [128, 512], F32)
        self.ps5 = self.psumt("ps5", [128, 512], F32)
        self.pbf = self.psumt("pbf", [128, 1024], BF16)
        self.prow = self.psumt("prow", [128, 512], F32)

        self._consts()
        self._weights()
        self.ph0()
        self.tap_sbuf("y1", lambda: self.xt.ap())
        stages = {"ph0": 0, "l0": 1, "inv1": 2, "x1": 3, "dense": 4, "x2": 5,
                  "inv2": 6, "full": 7}[BUILD_UPTO]
        if stages >= 1:
            for l in range(min(stages, 2)):
                self.ffn_layer(1, l)
        if stages >= 3:
            self.combine(self.dm["x0cm"], self.dm["x1cm"], 0.5, y_dst="y2cm")
        if stages >= 4:
            self.dense_attn()
        if stages >= 5:
            self.combine(self.dm["x1cm"], self.dm["x2cm"], 1.0,
                         osrc=self.dm["Ocm"], y_dst="xt")
        if stages >= 6:
            for l in range(2):
                self.ffn_layer(2, l, emit_final=(stages >= 7 and l == 1))
        if stages >= 7:
            pass
        else:
            self._emit_out_from_xt()
        self.sb.drain_all()
        self.stack.close()
        return self

    def _consts(self):
        sb, dr = self.sb, self.dr
        sb.op("vector", lambda e: e.memset(self.bstg2.ap(), 0.0),
              writes=[self.bstg2.ap()])
        sb.op("vector", lambda e: e.memset(self.bstg3.ap(), 0.0),
              writes=[self.bstg3.ap()])
        sb.dma("sync", self.flag[:, :], dr["flagcol"][:, :])
        sb.dma("sync", self.ident[:, :], dr["ident"][:, :])
        sb.dma("sync", self.identf[:, :], dr["identf"][:, :])
        sb.dma("sync", self.mask_std[:, :], dr["mask_std"][:, :])
        sb.dma("sync", self.mask_lo.ap(),
               dr["mask_lo"].ap().rearrange("(q p) f -> p q f", p=128))
        for l in range(2):
            for gb, name in ((0, "lng"), (1, "lnb")):
                k = l * 2 + gb
                stg = self.stat[0:8, k * 128:(k + 1) * 128]
                sb.dma("sync", stg, dr[name][l, :].rearrange("(p f) -> p f", f=128))
                pdst = self.prow[:, k * 8:k * 8 + 8]
                sb.op("tensor", lambda e, stg=stg, pdst=pdst: e.transpose(
                    pdst, stg, self.identf[0:8, 0:8]),
                    reads=[stg, self.identf[0:8, 0:8]], writes=[pdst])
                dstc = self.lgbc[:, l, gb, :]
                sb.op("vector", lambda e, dstc=dstc, pdst=pdst: e.tensor_copy(dstc, pdst),
                      reads=[pdst], writes=[dstc])
            sb.dma("sync", self.qsr[0:1, l, :], dr["qs"][l:l + 1, :])
            sb.dma("sync", self.ksr[0:1, l, :], dr["ks"][l:l + 1, :])
            for hf in range(2):
                off = hf * HID
                sb.dma("sync", self.bstg2[0:21, l, hf, 0:128],
                       dr["b1"][l, off:off + 2688].rearrange("(p f) -> p f", f=128))
                sb.dma("sync", self.bstg2[21:22, l, hf, 0:42],
                       dr["b1"][l, off + 2688:off + HID].unsqueeze(0))
            sb.dma("sync", self.bstg3[0:8, l, 0:128],
                   dr["b2"][l, :].rearrange("(p f) -> p f", f=128))
        sb.op("vector", lambda e: e.memset(self.browones.ap(), 1.0),
              writes=[self.browones.ap()])
        sb.op("vector", lambda e: e.memset(self.onesrow0[:, :], 1.0),
              writes=[self.onesrow0[:, :]])
        sb.op("vector", lambda e: e.memset(self.onescol[:, :], 1.0),
              writes=[self.onescol[:, :]])
        sb.op("vector", lambda e: e.memset(self.epsc[:, :], EPS),
              writes=[self.epsc[:, :]])
        sb.op("vector", lambda e: e.memset(self.epscol[:, :], EPS),
              writes=[self.epscol[:, :]])
        for l in range(2):
            for hf in range(2):
                src_ap = self.bstg2[0:NJ, l, hf, :]
                pdst = self.prow[:, hf * NJ:hf * NJ + NJ]
                sb.op("tensor", lambda e, src_ap=src_ap, pdst=pdst: e.transpose(
                    pdst, src_ap, self.identf[0:NJ, 0:NJ]),
                    reads=[src_ap, self.identf[0:NJ, 0:NJ]], writes=[pdst])
            sb.op("vector", lambda e, l=l: e.tensor_copy(self.b1c[:, l, :], self.prow[:, 0:2 * NJ]),
                  reads=[self.prow[:, 0:2 * NJ]], writes=[self.b1c[:, l, :]])
            src_ap = self.bstg3[0:NCH, l, :]
            pdst = self.prow[:, 64:64 + NCH]
            sb.op("tensor", lambda e, src_ap=src_ap, pdst=pdst: e.transpose(
                pdst, src_ap, self.identf[0:NCH, 0:NCH]),
                reads=[src_ap, self.identf[0:NCH, 0:NCH]], writes=[pdst])
            sb.op("vector", lambda e, l=l: e.tensor_copy(self.b2c[:, l, :], self.prow[:, 64:64 + NCH]),
                  reads=[self.prow[:, 64:64 + NCH]], writes=[self.b2c[:, l, :]])

    def _weights(self):
        sb, dr, dm = self.sb, self.dr, self.dm
        jobs = []

        def wchunk(sname, dname, l, r0, r1):
            jobs.append(((sname, l), lambda: sb.dma(
                "gpsimd", dm[sname][l][r0:r1, :], dr[dname][l][r0:r1, :])))

        def wdchunk(s):
            jobs.append((("s_Wd", 0), lambda: sb.dma(
                "gpsimd",
                dm["s_Wd"][s * D:(s + 1) * D, :].rearrange("(h d) c -> h d c", h=KH),
                dr["Wd"].ap().rearrange("(h s d) c -> s h d c", h=KH, s=3)[s])))

        for l in range(2):
            for r in range(0, D, 256):
                wchunk("s_w1", "w1", l, r, r + 256)
            for r0 in (0, 910, 1820):
                wchunk("s_w2", "w2", l, r0, min(r0 + 910, HID))
            if l == 0:
                for s in range(3):
                    wdchunk(s)
        self._wjobs = jobs

    def _drain_wjob(self, n=1):
        for _ in range(n):
            if self._wjobs:
                self._wjobs.pop(0)[1]()

    def _drain_wtag(self, tag):
        while any(t == tag for t, _ in self._wjobs):
            self._wjobs.pop(0)[1]()

    def ln_stats(self, src_fn, sqbuf, ci, w=512, layer=None):
        sb = self.sb
        b0 = (ci % 2) * 1024
        pSum = self.pD[:, b0:b0 + w]
        pSq = self.pD[:, b0 + 512:b0 + 512 + w]
        for ch in range(NCH):
            src = src_fn(ch)
            sb.op("scalar", lambda e, src=src: e.activation(sqbuf, src, AF.Square),
                  reads=[src], writes=[sqbuf])
            sb.op("tensor", lambda e, src=src, ch=ch: e.matmul(
                pSum[0:1, :], self.onescol[:, :], src,
                start=(ch == 0), stop=(ch == NCH - 1)),
                reads=[self.onescol[:, :], src], writes=[pSum[0:1, :]])
            sb.op("tensor", lambda e, ch=ch: e.matmul(
                pSq[0:1, :], self.onescol[:, :], sqbuf,
                start=(ch == 0), stop=(ch == NCH - 1)),
                reads=[self.onescol[:, :], sqbuf], writes=[pSq[0:1, :]])
        L = self.rows[0:1, ci % 2]
        bo = self.browones[:, ci % 2, :]
        cs = slice(0, w)
        sb.op("vector", lambda e, L=L, pSum=pSum: e.tensor_scalar(
            L[:, 0, cs], pSum[0:1, :], 1.0 / D, None, ALU.mult),
            reads=[pSum[0:1, :]], writes=[L[:, 0, cs]])
        sb.op("scalar", lambda e, L=L: e.activation(L[:, 2, cs], L[:, 0, cs], AF.Square),
              reads=[L[:, 0, cs]], writes=[L[:, 2, cs]])
        sb.op("vector", lambda e, L=L, pSq=pSq: e.scalar_tensor_tensor(
            L[:, 1, cs], pSq[0:1, :], 1.0 / D, L[:, 2, cs], ALU.mult, ALU.subtract),
            reads=[pSq[0:1, :], L[:, 2, cs]], writes=[L[:, 1, cs]])


    def ln_apply(self, src_fn, dst_fn, ci, w=512, layer=None):
        sb = self.sb
        pa, pb = self.ps4, self.ps5
        L = self.rows[0:1, ci % 2]
        cs = slice(0, w)
        ones = self.onesrow0[0:1, :]
        sb.op("tensor", lambda e: e.matmul(
            pa[:, 0:w], ones, L[:, 1, cs], start=True, stop=True),
            reads=[ones, L[:, 1, cs]], writes=[pa[:, 0:w]])
        sb.op("tensor", lambda e: e.matmul(
            pb[:, 0:w], ones, L[:, 0, cs], start=True, stop=True),
            reads=[ones, L[:, 0, cs]], writes=[pb[:, 0:w]])
        rst = self.stat[:, 1024 + (ci % 2) * 512:1024 + (ci % 2) * 512 + w]
        sb.op("scalar", lambda e, rst=rst: e.activation(
            rst, pa[:, 0:w], AF.Sqrt, bias=self.epscol[:, :]),
            reads=[pa[:, 0:w], self.epscol[:, :]], writes=[rst])
        sb.op("vector", lambda e, rst=rst: e.reciprocal(rst, rst),
              reads=[rst], writes=[rst])
        for ch in range(NCH):
            src = src_fn(ch)
            dst = dst_fn(ch)
            t1 = self.stat[:, (ch % 2) * 512:(ch % 2) * 512 + w]
            sb.op("vector", lambda e, src=src, t1=t1: e.tensor_tensor(
                t1, src, pb[:, 0:w], ALU.subtract),
                reads=[src, pb[:, 0:w]], writes=[t1])
            if layer is None:
                sb.op("vector", lambda e, dst=dst, t1=t1, rst=rst: e.tensor_tensor(
                    dst, t1, rst, ALU.mult),
                    reads=[t1, rst], writes=[dst])
            else:
                sb.op("vector", lambda e, t1=t1, rst=rst: e.tensor_tensor(
                    t1, t1, rst, ALU.mult),
                    reads=[t1, rst], writes=[t1])
                g = self.lgbc[:, layer, 0, ch:ch + 1]
                b = self.lgbc[:, layer, 1, ch:ch + 1]
                sb.op("scalar", lambda e, dst=dst, t1=t1, g=g, b=b: e.activation(
                    dst, t1, AF.Identity, bias=b, scale=g),
                    reads=[t1, g, b], writes=[dst])

    def ln_apply_bat(self, src3, dst3, tb, ci, nch=NCH):
        sb = self.sb
        pa, pb = self.ps4, self.ps5
        L = self.rows[0:1, ci % 2]
        ones = self.onesrow0[0:1, :]
        sb.op("tensor", lambda e: e.matmul(
            pa[:, 0:512], ones, L[:, 1, 0:512], start=True, stop=True),
            reads=[ones, L[:, 1, 0:512]], writes=[pa[:, 0:512]])
        sb.op("tensor", lambda e: e.matmul(
            pb[:, 0:512], ones, L[:, 0, 0:512], start=True, stop=True),
            reads=[ones, L[:, 0, 0:512]], writes=[pb[:, 0:512]])
        rst = self.stat[:, 1024 + (ci % 2) * 512:1536 + (ci % 2) * 512]
        sb.op("scalar", lambda e, rst=rst: e.activation(
            rst, pa[:, 0:512], AF.Sqrt, bias=self.epscol[:, :]),
            reads=[pa[:, 0:512], self.epscol[:, :]], writes=[rst])
        sb.op("vector", lambda e, rst=rst: e.reciprocal(rst, rst),
              reads=[rst], writes=[rst])
        pbB = pb[:, 0:512].unsqueeze(1).broadcast_to([128, nch, 512])
        rstB = rst.unsqueeze(1).broadcast_to([128, nch, 512])
        sb.op("vector", lambda e: e.tensor_tensor(tb, src3, pbB, ALU.subtract),
              reads=[src3, pb[:, 0:512]], writes=[tb])
        sb.op("vector", lambda e: e.tensor_tensor(dst3, tb, rstB, ALU.mult),
              reads=[tb, rst], writes=[dst3])

    def ph0(self):
        nc, sb, dr, dm = self.nc, self.sb, self.dr, self.dm
        self.ar.push()
        x0tm = self.ar.take([128, 2, D], F32)
        cmt = self.ar.take([128, 2, NCH, 4, 128], F32)
        sq = self.ar.take([128, 2, 512], F32)
        tb = self.ar.take([128, NCH, 512], F32)
        pT = self.prow
        pend = None
        for ci in range(4):
            cb = ci % 2
            for tt in range(4):
                t = ci * 4 + tt
                db = t % 2
                tm = x0tm[:, db, :]
                sb.dma("sync", tm, dr["x0"][t * 128:(t + 1) * 128, :])
                if tt == 0:
                    self._drain_wjob(1)
                for ch in range(NCH):
                    src = x0tm[:, db, ch * 128:(ch + 1) * 128]
                    sb.op("tensor",
                          lambda e, src=src: e.transpose(pT[:, 0:128], src, self.identf[:, :]),
                          reads=[src, self.identf[:, :]], writes=[pT[:, 0:128]])
                    dst = cmt[:, cb, ch, tt, :]
                    sb.op("vector", lambda e, dst=dst: e.tensor_copy(dst, pT[:, 0:128]),
                          reads=[pT[:, 0:128]], writes=[dst])
                    sb.dma("scalar",
                           dm["x0cm"][ch * 128:(ch + 1) * 128, t * 128:(t + 1) * 128], dst)
            self.ln_stats(lambda ch, cb=cb: cmt[:, cb, ch, :, :].rearrange("p a b -> p (a b)"),
                          sq[:, cb, :], ci)
            if pend is not None:
                self.ln_apply_bat(*pend)
            pend = (cmt[:, cb].rearrange("p c a b -> p c (a b)"),
                    self.xt[:, :, 1 + ci * 512:1 + ci * 512 + 512], tb, ci)
        self.ln_apply_bat(*pend)
        self.ar.pop()

    def ffn_layer(self, inv, l, emit_final=False):
        self.cc1()
        self.qkv_attn(l)
        self.wo_residual(l)
        self.cc2()
        self.ln_glu(l, emit_final)

    def cc1(self):
        sb, dm = self.sb, self.dm
        self._drain_wjob(1)
        self.ar.push()
        stg = self.ar.take([128, HALOT], BF16)
        for ch in range(NCH):
            src = self.xt[:, ch, 1 + T - HALOT:1 + T]
            sb.op("scalar", lambda e, src=src, stg=stg: e.copy(stg, src),
                  reads=[src], writes=[stg])
            sb.dma("scalar", dm["cc1_in"][ch * 128:(ch + 1) * 128, :], stg)
        sb.op("gpsimd", lambda e: e.collective_compute(
            "AllGather", ALU.bypass,
            replica_groups=[[0, 1], [2, 3], [4, 5], [6, 7]],
            ins=[dm["cc1_in"].ap()], outs=[dm["cc1_out"].ap()]),
            reads=[dm["cc1_in"].ap()], writes=[dm["cc1_out"].ap()], kind="cc")
        for ch in range(NCH):
            sb.dma("gpsimd", self.xthalo[:, ch, :],
                   dm["cc1_out"][ch * 128:(ch + 1) * 128, :])
        for ch in range(NCH):
            src = self.xthalo[:, ch, HALOT - 1:HALOT]
            dst = self.xt[:, ch, 0:1]
            sb.op("vector", lambda e, src=src, dst=dst: e.tensor_tensor(
                dst, src, self.flag[:, :], ALU.mult),
                reads=[src, self.flag[:, :]], writes=[dst])
        self.ar.pop()

    def qkv_attn(self, l):
        nc, sb, dr, dm = self.nc, self.sb, self.dr, self.dm
        self.ar.push()
        h_bf = self.ar.take([128, NCH, T], BF16)
        wslab = self.ar.take([128, 1, NCH, 3, 128], BF16)
        q_cm = self.ar.take([128, T], BF16)
        k_cm = self.ar.take([128, HALO + T], BF16)
        v_tm = self.ar.take([128, 4 + NT, 2, 65], BF16)
        qk4 = self.ar.take([128, 2, 4, 2, 128], F32)
        t24 = self.ar.take([128, 8, 2, DH], F32)
        t34 = self.stat[:, 512:1536].rearrange("p (a h d) -> p a h d", a=8, h=2)
        tabs = self.ar.take([128, 2, 8, 2, DH], BF16)
        qb16 = self.ar.take([128, 2, 8, 2, DH], BF16)
        qsbig = self.ar.take([128, 2, 8, DH], BF16)
        rn4 = self.ar.take([128, 16], F32)
        oacc = self.ar.take([128, 2, 4, 128], F32)
        obuf = self.ar.take([128, 2, 512], BF16)
        sq4 = self.stat[:, 512:1536]

        for (row, bidx) in ((self.qsr, 0), (self.ksr, 2)):
            src = row[0:1, l, :]
            pd = self.prow[:, 0:DH]
            sb.op("tensor", lambda e, src=src, pd=pd: e.matmul(
                pd, self.onesrow0[0:1, :], src, start=True, stop=True),
                reads=[self.onesrow0[0:1, :], src], writes=[pd])
            d0 = self.qsb[:, bidx, :]
            sb.op("vector", lambda e, d0=d0, pd=pd: e.tensor_copy(d0, pd),
                  reads=[pd], writes=[d0])
            d1 = self.qsb[:, bidx + 1, :]
            sb.op("vector", lambda e, d1=d1, d0=d0: e.tensor_copy(d1[:, 0:32], d0[:, 32:64]),
                  reads=[d0[:, 32:64]], writes=[d1[:, 0:32]])
            sb.op("vector", lambda e, d1=d1, d0=d0: e.tensor_copy(d1[:, 32:64], d0[:, 0:32]),
                  reads=[d0[:, 0:32]], writes=[d1[:, 32:64]])
        for v in range(2):
            for s in range(2):
                srcq = self.qsb[:, s * 2 + v, :].unsqueeze(1).broadcast_to([128, 4, DH])
                dstq = qsbig[:, v].rearrange("p (t s) d -> p t s d", s=2)[:, :, s, :]
                sb.op("vector", lambda e, srcq=srcq, dstq=dstq: e.tensor_copy(dstq, srcq),
                      reads=[self.qsb.ap()], writes=[dstq])
        for ch in range(NCH):
            shift = 0 if ch < 4 else 1
            src = self.xt[:, ch, 1 - shift:1 - shift + T]
            dst = h_bf[:, ch, :]
            sb.op("scalar", lambda e, src=src, dst=dst: e.copy(dst, src),
                  reads=[src], writes=[dst])

        def load_wslab(hp):
            for s in range(3):
                colofs = s * 512 + hp * 128
                sb.dma("gpsimd", wslab[:, 0, :, s, :],
                       dr["wqkv"][l][:, colofs:colofs + 128]
                       .rearrange("(c p) f -> p c f", p=128))

        load_wslab(0)
        for hp in range(HEADS // 2):
            self._drain_wjob(1)
            pend_tail = []
            for gi, g0 in enumerate(list(range(0, NT, 4)) + [-4]):
                halo = g0 < 0
                gp = gi % 2
                trow = (g0 + 4) * 128
                for s in (0, 1):
                    tc = dr[("cq", "ck")[s]][trow:trow + 512, :]
                    ts_ = dr[("sq", "sk")[s]][trow:trow + 512, :]
                    tdst = tabs[:, gp].rearrange("p (t s) c d -> p t s c d", s=2)
                    sb.dma("gpsimd", tdst[:, :, s, 0, :],
                           tc.rearrange("(a p) f -> p a f", p=128))
                    sb.dma("gpsimd", tdst[:, :, s, 1, :],
                           ts_.rearrange("(a p) f -> p a f", p=128))
                for ti in range(4):
                    t = g0 + ti
                    pqkv = self.pD[:, 0:384] if ti % 2 == 0 else self.pD[:, 512:896]
                    for ch in range(NCH):
                        if halo:
                            shift = 0 if ch < 4 else 1
                            cb = 1 - shift + (t + 4) * 128
                            hh = self.xthalo[:, ch, cb:cb + 128]
                        else:
                            hh = h_bf[:, ch, t * 128:(t + 1) * 128]
                        rhs = wslab[:, 0, ch, :, :].rearrange("p a b -> p (a b)")
                        sb.op("tensor", lambda e, hh=hh, rhs=rhs, pqkv=pqkv, ch=ch: e.matmul(
                            pqkv, hh, rhs, start=(ch == 0), stop=(ch == NCH - 1)),
                            reads=[hh, rhs], writes=[pqkv])
                    for side in range(2):
                        dstq = qk4[:, gp, ti, side, :]
                        srcp = pqkv[:, side * 128:side * 128 + 128]
                        sb.op("scalar", lambda e, dstq=dstq, srcp=srcp: e.copy(dstq, srcp),
                              reads=[srcp], writes=[dstq])
                    vdst = v_tm[:, t + 4, :, 0:64]
                    vsrc = pqkv[:, 256:384].rearrange("p (h d) -> p h d", h=2)
                    sb.op("scalar", lambda e, vdst=vdst, vsrc=vsrc: e.copy(vdst, vsrc),
                          reads=[pqkv[:, 256:384]], writes=[vdst])
                    vones = v_tm[:, t + 4, :, 64:65]
                    sb.op("vector", lambda e, vones=vones: e.memset(vones, 1.0),
                          writes=[vones])
                qkg = qk4[:, gp]
                qall = qkg.rearrange("p a b c -> p (a b c)")
                qh = qkg.rearrange("p a b (h d) -> p (a b) h d", h=2)
                sb.op("scalar", lambda e, qall=qall: e.activation(sq4, qall, AF.Square),
                      reads=[qkg], writes=[sq4])
                sqh = sq4.rearrange("p (g d) -> p g d", d=DH)
                sb.op("vector", lambda e, sqh=sqh: e.reduce_sum(
                    rn4[:, :], sqh, axis=AX.X),
                    reads=[sq4], writes=[rn4[:, :]])
                sb.op("scalar", lambda e: e.activation(rn4[:, :], rn4[:, :], AF.Sqrt),
                      reads=[rn4[:, :]], writes=[rn4[:, :]])
                sb.op("vector", lambda e: e.tensor_scalar(
                    rn4[:, :], rn4[:, :], 1e-12, None, ALU.max),
                    reads=[rn4[:, :]], writes=[rn4[:, :]])
                sb.op("vector", lambda e: e.reciprocal(rn4[:, :], rn4[:, :]),
                      reads=[rn4[:, :]], writes=[rn4[:, :]])
                tt0 = tabs[:, gp, :, 0, :]
                tt1 = tabs[:, gp, :, 1, :]
                sb.op("vector", lambda e, tt0=tt0: e.tensor_tensor(
                    tt0, tt0, qsbig[:, 0], ALU.mult),
                    reads=[tt0, qsbig[:, 0]], writes=[tt0])
                sb.op("vector", lambda e, tt1=tt1: e.tensor_tensor(
                    tt1, tt1, qsbig[:, 1], ALU.mult),
                    reads=[tt1, qsbig[:, 1]], writes=[tt1])
                ctab = tt0.unsqueeze(2).broadcast_to([128, 8, 2, DH])
                stab = tt1.unsqueeze(2).broadcast_to([128, 8, 2, DH])
                sb.op("vector", lambda e, qh=qh, ctab=ctab: e.tensor_tensor(
                    t24, qh, ctab, ALU.mult),
                    reads=[qkg, tabs[:, gp]], writes=[t24])
                sb.op("vector", lambda e, qh=qh, stab=stab: e.tensor_tensor(
                    t34[:, :, :, 0:32], qh[:, :, :, 32:64], stab[:, :, :, 0:32], ALU.mult),
                    reads=[qkg, tabs[:, gp]], writes=[t34])
                sb.op("vector", lambda e, qh=qh, stab=stab: e.tensor_tensor(
                    t34[:, :, :, 32:64], qh[:, :, :, 0:32], stab[:, :, :, 32:64], ALU.mult),
                    reads=[qkg, tabs[:, gp]], writes=[t34])
                sb.op("vector", lambda e: e.tensor_tensor(
                    t24, t24, t34, ALU.add),
                    reads=[t24, t34], writes=[t24])
                rb = rn4.rearrange("p (a h) -> p a h", h=2).unsqueeze(-1) \
                    .broadcast_to([128, 8, 2, DH])
                qbo = qb16[:, gp]
                sb.op("vector", lambda e, rb=rb, qbo=qbo: e.tensor_tensor(
                    qbo, t24, rb, ALU.mult),
                    reads=[t24, rn4], writes=[qbo])
                for fn in pend_tail:
                    fn()
                pend_tail = []
                for side in ([1] if halo else [0, 1]):
                    def tail(side=side, g3=gp, g0=g0):
                        pb0 = side * 512
                        for ti in range(4):
                            srcb = qb16[:, g3, ti * 2 + side].rearrange("p a b -> p (a b)")
                            sb.op("tensor", lambda e, srcb=srcb, ti=ti, pb0=pb0: e.transpose(
                                self.pbf[:, pb0 + ti * 128:pb0 + (ti + 1) * 128], srcb,
                                self.ident[:, :]),
                                reads=[srcb, self.ident[:, :]],
                                writes=[self.pbf[:, pb0 + ti * 128:pb0 + (ti + 1) * 128]])
                        if side == 0:
                            dcol = q_cm[:, g0 * 128:(g0 + 4) * 128]
                        else:
                            dcol = k_cm[:, (g0 + 4) * 128:(g0 + 8) * 128]
                        sb.op("scalar", lambda e, dcol=dcol, pb0=pb0: e.copy(
                            dcol, self.pbf[:, pb0:pb0 + 512]),
                            reads=[self.pbf[:, pb0:pb0 + 512]], writes=[dcol])
                    pend_tail.append(tail)
            if hp < HEADS // 2 - 1:
                load_wslab(hp + 1)
            for fn in pend_tail:
                fn()

            units = [(hh, qc) for hh in range(2)
                     for qc in list(range(4, NT)) + list(range(4))]

            def la_scores(u, hh, qc):
                base = hh * 64
                pS = self.pD[:, 0:1024] if u % 2 == 0 else self.pD[:, 1024:2048]
                lq = q_cm[base:base + 64, qc * 128:(qc + 1) * 128]
                k0 = qc * 128
                kdiag = k_cm[base:base + 64, k0 + 512:k0 + 640]
                kfar = k_cm[base:base + 64, k0:k0 + 384]
                kf3 = k_cm[base:base + 64, k0 + 384:k0 + 512]
                sb.op("tensor", lambda e, lq=lq, kdiag=kdiag, pS=pS: e.matmul(
                    pS[:, 0:128], lq, kdiag, start=True, stop=True),
                    reads=[lq, kdiag], writes=[pS[:, 0:128]])
                sb.op("tensor", lambda e, lq=lq, kfar=kfar, pS=pS: e.matmul(
                    pS[:, 128:512], lq, kfar, start=True, stop=True),
                    reads=[lq, kfar], writes=[pS[:, 128:512]])
                sb.op("tensor", lambda e, lq=lq, kf3=kf3, pS=pS: e.matmul(
                    pS[:, 512:640], lq, kf3, start=True, stop=True),
                    reads=[lq, kf3], writes=[pS[:, 512:640]])

            def la_softmax(u, hh, qc):
                pS = self.pD[:, 0:1024] if u % 2 == 0 else self.pD[:, 1024:2048]
                if qc < 4:
                    m0 = self.mask_lo[:, qc, :]
                    sb.op("vector", lambda e, m0=m0, pS=pS: e.tensor_tensor(
                        pS[:, 0:640], pS[:, 0:640], m0, ALU.add),
                        reads=[pS[:, 0:640], m0], writes=[pS[:, 0:640]])
                else:
                    m0 = self.mask_std[:, 0:256]
                    sb.op("vector", lambda e, m0=m0, pS=pS: e.tensor_tensor(
                        pS[:, 0:256], pS[:, 0:256], m0, ALU.add),
                        reads=[pS[:, 0:256], m0], writes=[pS[:, 0:256]])
                ng = self.negm[:, u % 2:u % 2 + 1]
                sb.op("vector", lambda e, pS=pS, ng=ng: e.reduce_max(
                    ng, pS[:, 0:640], axis=AX.X, negate=True),
                    reads=[pS[:, 0:640]], writes=[ng])
                pq = self.Pq[:, u % 3, :]
                sb.op("scalar", lambda e, pS=pS, ng=ng, pq=pq: e.activation(
                    pq, pS[:, 0:640], AF.Exp, bias=ng),
                    reads=[pS[:, 0:640], ng], writes=[pq])

            def la_trans(u, hh, qc):
                pq = self.Pq[:, u % 3, :]
                for j in range(5):
                    src = pq[:, j * 128:(j + 1) * 128]
                    sb.op("tensor", lambda e, src=src, j=j: e.transpose(
                        self.pbf[:, j * 128:(j + 1) * 128], src, self.ident[:, :]),
                        reads=[src, self.ident[:, :]],
                        writes=[self.pbf[:, j * 128:(j + 1) * 128]])
                ptb = self.PT[:, u % 2, :]
                sb.op("vector", lambda e, ptb=ptb: e.tensor_copy(ptb, self.pbf[:, 0:640]),
                      reads=[self.pbf[:, 0:640]], writes=[ptb])

            def la_av(u, hh, qc):
                pO = self.ps4 if u % 2 == 0 else self.ps5
                tidx = [qc + 4, qc, qc + 1, qc + 2, qc + 3]
                for j in range(5):
                    lv = v_tm[:, tidx[j], hh, :]
                    rp = self.PT[:, u % 2, j * 128:(j + 1) * 128]
                    sb.op("tensor", lambda e, lv=lv, rp=rp, pO=pO, j=j: e.matmul(
                        pO[0:65, 0:128], lv, rp, start=(j == 0), stop=(j == 4)),
                        reads=[lv, rp], writes=[pO[0:65, 0:128]])

            def la_osave(u, hh, qc):
                pO = self.ps4 if u % 2 == 0 else self.ps5
                b = (u // 4) % 2
                k = u % 4
                rr = self.stat[0:1, b * 512 + k * 128:b * 512 + k * 128 + 128]
                sb.op("scalar", lambda e, rr=rr, pO=pO: e.copy(rr, pO[64:65, 0:128]),
                      reads=[pO[64:65, 0:128]], writes=[rr])
                dst = oacc[0:64, b, k, :]
                sb.op("vector", lambda e, dst=dst, pO=pO: e.tensor_copy(dst, pO[0:64, 0:128]),
                      reads=[pO[0:64, 0:128]], writes=[dst])

            def la_flush(u, hh, qc):
                h = hp * 2 + hh
                b = (u // 4) % 2
                qc0 = qc - 3
                rrow = self.stat[0:1, b * 512:b * 512 + 512]
                sb.op("vector", lambda e, rrow=rrow: e.reciprocal(rrow, rrow),
                      reads=[rrow], writes=[rrow])
                pBC = self.prow
                sb.op("tensor", lambda e, rrow=rrow, pBC=pBC: e.matmul(
                    pBC[0:64, 0:512], self.onesrow0[0:1, 0:64], rrow,
                    start=True, stop=True),
                    reads=[self.onesrow0[0:1, 0:64], rrow], writes=[pBC[0:64, 0:512]])
                bcs = self.stat[0:64, 1024 + b * 512:1536 + b * 512]
                sb.op("scalar", lambda e, bcs=bcs, pBC=pBC: e.copy(bcs, pBC[0:64, 0:512]),
                      reads=[pBC[0:64, 0:512]], writes=[bcs])
                ov = oacc[0:64, b].rearrange("p a b -> p (a b)")
                ob = obuf[0:64, b, :]
                sb.op("gpsimd", lambda e, ov=ov, ob=ob, bcs=bcs: e.tensor_tensor(
                    ob, ov, bcs, ALU.mult),
                    reads=[ov, bcs], writes=[ob])
                sb.dma("sync", dm["Olcm"][h * 64:(h + 1) * 64,
                                          qc0 * 128:qc0 * 128 + 512], ob)

            U = len(units)
            for u in range(U + 4):
                if u < U:
                    la_scores(u, *units[u])
                if 0 <= u - 1 < U:
                    la_softmax(u - 1, *units[u - 1])
                if 0 <= u - 2 < U:
                    la_trans(u - 2, *units[u - 2])
                    la_av(u - 2, *units[u - 2])
                if 0 <= u - 3 < U:
                    la_osave(u - 3, *units[u - 3])
                    if (u - 3) % 4 == 3:
                        la_flush(u - 3, *units[u - 3])
        self.ar.pop()

    def wo_residual(self, l):
        sb, dm, dr = self.sb, self.dm, self.dr
        self.ar.push()
        wot = self.ar.take([128, 4, NCH, 128], BF16)
        ob = self.ar.take([128, 2, 4, 512], BF16)
        for ic in range(4):
            sb.dma("gpsimd", wot[:, ic, :, :],
                   dr["wo"][l][ic * 128:(ic + 1) * 128, :].rearrange("p (o f) -> p o f", f=128))
        for ci in range(4):
            c0 = ci * 512
            for ic in range(4):
                sb.dma("gpsimd" if ic % 2 else "sync", ob[:, ci % 2, ic, :],
                       dm["Olcm"][ic * 128:(ic + 1) * 128, c0:c0 + 512])
            for oc in range(NCH):
                po = self.ps4 if oc % 2 == 0 else self.ps5
                for ic in range(4):
                    lw = wot[:, ic, oc, :]
                    rr = ob[:, ci % 2, ic, :]
                    sb.op("tensor", lambda e, lw=lw, rr=rr, po=po, ic=ic: e.matmul(
                        po[:, 0:512], lw, rr, start=(ic == 0), stop=(ic == 3)),
                        reads=[lw, rr], writes=[po[:, 0:512]])
                dst = self.xt[:, oc, 1 + c0:1 + c0 + 512]
                sb.op("vector", lambda e, dst=dst, po=po: e.tensor_tensor(
                    dst, dst, po[:, 0:512], ALU.add),
                    reads=[dst, po[:, 0:512]], writes=[dst])
        self.ar.pop()

    def cc2(self):
        sb, dm = self.sb, self.dm
        stg = self.stat[:, 0:8]
        stg2 = self.stat[:, 8:16]
        for ch in range(NCH):
            src = self.xt[:, ch, T:T + 1]
            dst = stg[:, ch:ch + 1]
            sb.op("scalar", lambda e, src=src, dst=dst: e.copy(dst, src),
                  reads=[src], writes=[dst])
        sb.dma("scalar", dm["cc2_in"].ap(), stg)
        sb.op("gpsimd", lambda e: e.collective_compute(
            "AllGather", ALU.bypass,
            replica_groups=[[0, 1], [2, 3], [4, 5], [6, 7]],
            ins=[dm["cc2_in"].ap()], outs=[dm["cc2_out"].ap()]),
            reads=[dm["cc2_in"].ap()], writes=[dm["cc2_out"].ap()], kind="cc")
        sb.dma("gpsimd", stg2, dm["cc2_out"][0:128, :])
        fb = self.flag[:, 0:1].broadcast_to([128, 8])
        sb.op("vector", lambda e, fb=fb: e.tensor_tensor(stg2, stg2, fb, ALU.mult),
              reads=[stg2, self.flag[:, :]], writes=[stg2])
        for ch in range(NCH):
            src = stg2[:, ch:ch + 1]
            dst = self.xt[:, ch, 0:1]
            sb.op("scalar", lambda e, src=src, dst=dst: e.copy(dst, src),
                  reads=[src], writes=[dst])

    def ln_glu(self, l, emit_final=False):
        sb, dm = self.sb, self.dm
        self._drain_wtag(("s_w1", l))
        self._drain_wtag(("s_w2", l))
        self.ar.push()
        ln_h = self.ar.take([128, NCH, T], BF16)
        glu = self.ar.take([128, NJ, 512], BF16)
        wa = self.ar.take([128, 2, NCH, 128], BF16)
        wg = self.ar.take([128, 2, NCH, 128], BF16)
        w2t = self.ar.take([128, 2, 4, 256], BF16)
        sqb = self.ar.take([128, 2, 512], F32)
        gg = self.ar.take([128, 2, 512], F32)

        def lsrc(ch, c0):
            shift = 0 if ch < 4 else 1
            return self.xt[:, ch, 1 - shift + c0:1 - shift + c0 + 512]

        pend = None
        for ci in range(4):
            c0 = ci * 512
            self.ln_stats(lambda ch, c0=c0: lsrc(ch, c0), sqb[:, ci % 2, :], ci, layer=l)
            if pend is not None:
                self.ln_apply(*pend)
            pend = (lambda ch, c0=c0: lsrc(ch, c0),
                    lambda ch, c0=c0: ln_h[:, ch, c0:c0 + 512], ci, 512, l)
        self.ln_apply(*pend)

        for ci in range(4):
            self._drain_wjob(1)
            c0 = ci * 512
            for j in range(NJ):
                mj = 128 if j < NJ - 1 else HID - 128 * (NJ - 1)
                db = j % 2
                sb.dma("sync", wa[:, db, :, 0:mj],
                       dm["s_w1"][l][:, j * 128:j * 128 + mj]
                       .rearrange("(c p) f -> p c f", p=128))
                sb.dma("gpsimd", wg[:, db, :, 0:mj],
                       dm["s_w1"][l][:, HID + j * 128:HID + j * 128 + mj]
                       .rearrange("(c p) f -> p c f", p=128))
                b0 = db * 1024
                pA = self.pD[:, b0:b0 + 512]
                pG = self.pD[:, b0 + 512:b0 + 1024]
                for ch in range(NCH):
                    lr = ln_h[:, ch, c0:c0 + 512]
                    la = wa[:, db, ch, 0:mj]
                    lg_ = wg[:, db, ch, 0:mj]
                    sb.op("tensor", lambda e, la=la, lr=lr, pA=pA, ch=ch, mj=mj: e.matmul(
                        pA[0:mj, :], la, lr, start=(ch == 0), stop=(ch == NCH - 1)),
                        reads=[la, lr], writes=[pA[0:mj, :]])
                    sb.op("tensor", lambda e, lg_=lg_, lr=lr, pG=pG, ch=ch, mj=mj: e.matmul(
                        pG[0:mj, :], lg_, lr, start=(ch == 0), stop=(ch == NCH - 1)),
                        reads=[lg_, lr], writes=[pG[0:mj, :]])
                bg = self.b1c[0:mj, l, NJ + j:NJ + j + 1]
                ba = self.b1c[0:mj, l, j:j + 1]
                ggj = gg[0:mj, db, :]
                sb.op("scalar", lambda e, pG=pG, bg=bg, mj=mj, ggj=ggj: e.activation(
                    ggj, pG[0:mj, :], AF.Gelu, bias=bg),
                    reads=[pG[0:mj, :], bg], writes=[ggj])
                gj = glu[0:mj, j, :]
                sb.op("vector", lambda e, gj=gj, pA=pA, ba=ba, mj=mj, ggj=ggj:
                      e.scalar_tensor_tensor(
                          gj, pA[0:mj, :], ba, ggj, ALU.add, ALU.mult),
                      reads=[pA[0:mj, :], ba, ggj], writes=[gj])
            for pp in range(4):
                def w2load(qi, pp=pp):
                    j0 = qi * 4
                    db = qi % 2
                    if j0 + 4 <= NJ - 1 or (j0 + 4 == NJ and HID % 128 == 0):
                        sb.dma("scalar", w2t[:, db, :, :],
                               dm["s_w2"][l][j0 * 128:j0 * 128 + 512,
                                             pp * 256:pp * 256 + 256]
                               .rearrange("(j p) f -> p j f", p=128))
                    else:
                        for j in range(j0, NJ):
                            mj = 128 if j < NJ - 1 else HID - 128 * (NJ - 1)
                            sb.dma("scalar", w2t[0:mj, db, j - j0, :],
                                   dm["s_w2"][l][j * 128:j * 128 + mj,
                                                 pp * 256:pp * 256 + 256])
                nq = (NJ + 3) // 4
                w2load(0)
                for qi in range(nq):
                    if qi + 1 < nq:
                        w2load(qi + 1)
                    for j in range(qi * 4, min(qi * 4 + 4, NJ)):
                        mj = 128 if j < NJ - 1 else HID - 128 * (NJ - 1)
                        db = qi % 2
                        gj = glu[0:mj, j, :]
                        for oo in range(2):
                            lw = w2t[0:mj, db, j - qi * 4, oo * 128:(oo + 1) * 128]
                            pm = self.ps4 if oo == 0 else self.ps5
                            sb.op("tensor", lambda e, lw=lw, gj=gj, pm=pm, j=j, mj=mj: e.matmul(
                                pm[:, 0:512], lw, gj, start=(j == 0), stop=(j == NJ - 1)),
                                reads=[lw, gj], writes=[pm[:, 0:512]])
                for oo in range(2):
                    oc = pp * 2 + oo
                    pm = self.ps4 if oo == 0 else self.ps5
                    bb = self.b2c[:, l, oc:oc + 1]
                    dst = self.xt[:, oc, 1 + c0:1 + c0 + 512]
                    sb.op("vector", lambda e, pm=pm, bb=bb, dst=dst: e.scalar_tensor_tensor(
                        dst, pm[:, 0:512], bb, dst, ALU.add, ALU.add),
                        reads=[pm[:, 0:512], bb, dst], writes=[dst])
            if emit_final:
                fxc = gg.rearrange("p a b -> p (a b)").rearrange(
                    "p (c f) -> p c f", f=128)
                self.final_ci(ci, fxc, sqb[:, ci % 2, :])
        self.ar.pop()

    def combine(self, xprev, xnext, scale, osrc=None, y_dst="y2cm"):
        sb, dm = self.sb, self.dm
        self.ar.push()
        xc = self.ar.take([128, NCH, 512], F32)
        x1c8 = self.ar.take([128, 2, NCH, 512], F32)
        y2b = self.ar.take([128, NCH, 512], BF16)
        sqb = self.ar.take([128, 2, 512], F32)
        tb = self.ar.take([128, NCH, 512], F32)
        pend = None
        for ci in range(4):
            c0 = ci * 512
            cb = ci % 2
            for ch in range(NCH):
                sb.dma("sync" if ch % 2 else "scalar", xc[:, ch, :],
                       xprev[ch * 128:(ch + 1) * 128, c0:c0 + 512])
                if osrc is not None:
                    sb.dma("gpsimd", tb[:, ch, :],
                           osrc[ch * 128:(ch + 1) * 128, c0:c0 + 512])
            dst = x1c8[:, cb]
            if osrc is None:
                src2 = self.xt[:, :, 1 + c0:1 + c0 + 512]
                sb.op("vector", lambda e, dst=dst, src2=src2: e.scalar_tensor_tensor(
                    dst, src2, scale, xc[:, :, :], ALU.mult, ALU.add),
                    reads=[src2, xc[:, :, :]], writes=[dst])
            else:
                sb.op("vector", lambda e, dst=dst: e.tensor_tensor(
                    dst, tb[:, :, :], xc[:, :, :], ALU.add),
                    reads=[tb[:, :, :], xc[:, :, :]], writes=[dst])
            for ch in range(NCH):
                sb.dma("scalar" if ch % 2 else "sync",
                       xnext[ch * 128:(ch + 1) * 128, c0:c0 + 512], x1c8[:, cb, ch, :])
            self.ln_stats(lambda ch, cb=cb: x1c8[:, cb, ch, :], sqb[:, cb, :], ci)
            if pend is not None:
                self._combine_apply(pend, y_dst, dm, tb)
            if y_dst == "y2cm":
                pend = (x1c8[:, cb], y2b[:, :, :], ci, c0)
            else:
                pend = (x1c8[:, cb], self.xt[:, :, 1 + c0:1 + c0 + 512], ci, c0)
        self._combine_apply(pend, y_dst, dm, tb)
        self.ar.pop()

    def _combine_apply(self, pend, y_dst, dm, tb):
        sb = self.sb
        src3, dst3, ci, c0 = pend
        self.ln_apply_bat(src3, dst3, tb[:, :, :], ci)
        if y_dst == "y2cm":
            k = c0 // 512
            for ch in range(NCH):
                sb.dma("sync" if ch % 2 else "gpsimd",
                       dm[f"y2c{k}"][ch * 128:(ch + 1) * 128, :],
                       dst3[:, ch, :])
            sb.op("gpsimd", lambda e, k=k: e.collective_compute(
                "AllGather", ALU.bypass,
                replica_groups=[[0, 1], [2, 3], [4, 5], [6, 7]],
                ins=[dm[f"y2c{k}"].ap()], outs=[dm[f"y2f{k}"].ap()]),
                reads=[dm[f"y2c{k}"].ap()], writes=[dm[f"y2f{k}"].ap()], kind="cc")

    def dense_attn(self):
        sb, dm = self.sb, self.dm
        self._drain_wtag(("s_Wd", 0))
        self.ar.push()
        Wqt = self.ar.take([128, 2, NCH, 128], BF16)
        Wkt = self.ar.take([128, 2, NCH, 128], BF16)
        Wvt = self.ar.take([128, 2, NCH, 128], BF16)
        Q_hp = self.ar.take([128, T], BF16)
        K_hp = self.ar.take([128, 2 * T], BF16)
        V_cm = self.ar.take([128, 2, 512], BF16)
        V_hp = self.ar.take([128, 32, 2, 65], BF16)
        y2c = self.ar.take([128, 2, NCH, 512], BF16)
        PTD = self.ar.take([128, 2, 1024], BF16)
        ost = self.ar.take([128, 2, 512], F32)
        rcp = self.ar.take([128, 2, 1024 + 512], F32)
        def wload(hp):
            wb = hp % 2
            for ch in range(NCH):
                sb.dma("sync", Wqt[:, wb, ch, :],
                       dm["s_Wd"][hp * 128:hp * 128 + 128, ch * 128:(ch + 1) * 128],
                       transpose=True)
                sb.dma("sync", Wkt[:, wb, ch, :],
                       dm["s_Wd"][D + hp * 128:D + hp * 128 + 128, ch * 128:(ch + 1) * 128],
                       transpose=True)
                sb.dma("sync", Wvt[:, wb, ch, :],
                       dm["s_Wd"][2 * D + hp * 128:2 * D + hp * 128 + 128,
                                  ch * 128:(ch + 1) * 128],
                       transpose=True)

        wload(0)
        for hp in range(KH // 2):
            wb = hp % 2
            for cj in range(12):
                db = cj % 2
                if cj < 4:
                    c0 = cj * 512
                    for ch in range(NCH):
                        sb.dma("gpsimd" if ch % 2 else "sync", y2c[:, db, ch, :],
                               dm[f"y2c{cj}"][ch * 128:(ch + 1) * 128, :])
                    wt, dst = Wqt[:, wb], Q_hp[:, c0:c0 + 512]
                else:
                    tok0 = (cj - 4) * 512
                    slot, kch = tok0 // T, (tok0 % T) // 512
                    for ch in range(NCH):
                        sb.dma("gpsimd" if ch % 2 else "sync", y2c[:, db, ch, :],
                               dm[f"y2f{kch}"][slot * D + ch * 128:
                                               slot * D + (ch + 1) * 128, :])
                    wt, dst = Wkt[:, wb], K_hp[:, tok0:tok0 + 512]
                pk = self.pD[:, 0:512] if db == 0 else self.pD[:, 512:1024]
                for ch in range(NCH):
                    lw = wt[:, ch, :]
                    rr = y2c[:, db, ch, :]
                    sb.op("tensor", lambda e, lw=lw, rr=rr, pk=pk, ch=ch: e.matmul(
                        pk, lw, rr, start=(ch == 0), stop=(ch == NCH - 1)),
                        reads=[lw, rr], writes=[pk])
                sb.op("scalar", lambda e, dst=dst, pk=pk: e.copy(dst, pk),
                      reads=[pk], writes=[dst])
                if cj >= 4:
                    pv = self.pD[:, 1024:1536] if db == 0 else self.pD[:, 1536:2048]
                    for ch in range(NCH):
                        lw = Wvt[:, wb, ch, :]
                        rr = y2c[:, db, ch, :]
                        sb.op("tensor", lambda e, lw=lw, rr=rr, pv=pv, ch=ch: e.matmul(
                            pv, lw, rr, start=(ch == 0), stop=(ch == NCH - 1)),
                            reads=[lw, rr], writes=[pv])
                    vc = V_cm[:, db, :]
                    sb.op("scalar", lambda e, vc=vc, pv=pv: e.copy(vc, pv),
                          reads=[pv], writes=[vc])
                    for tt in range(4):
                        src = V_cm[:, db, tt * 128:(tt + 1) * 128]
                        sb.op("tensor", lambda e, src=src, tt=tt: e.transpose(
                            self.pbf[:, tt * 128:(tt + 1) * 128], src, self.ident[:, :]),
                            reads=[src, self.ident[:, :]],
                            writes=[self.pbf[:, tt * 128:(tt + 1) * 128]])
                    kc0 = (cj - 4) * 4
                    for tt in range(4):
                        vd = V_hp[:, kc0 + tt, :, 0:64]
                        psrc = self.pbf[:, tt * 128:(tt + 1) * 128] \
                            .rearrange("p (h d) -> p h d", h=2)
                        sb.op("vector", lambda e, vd=vd, psrc=psrc: e.tensor_copy(vd, psrc),
                              reads=[self.pbf[:, tt * 128:(tt + 1) * 128]], writes=[vd])
                        vones = V_hp[:, kc0 + tt, :, 64:65]
                        sb.op("vector", lambda e, vones=vones: e.memset(vones, 1.0),
                              writes=[vones])
            if hp < KH // 2 - 1:
                wload(hp + 1)
            for hh in range(2):
                h = hp * 2 + hh
                base = hh * 64

                def d_scores(p, q0):
                    b0 = (p % 2) * 1024
                    lq = Q_hp[base:base + 64, q0:q0 + 512]
                    for o in range(2):
                        kc = 2 * p + o
                        lk = K_hp[base:base + 64, kc * 128:(kc + 1) * 128]
                        dstp = self.pD[:, b0 + o * 512:b0 + o * 512 + 512]
                        sb.op("tensor", lambda e, lk=lk, lq=lq, dstp=dstp: e.matmul(
                            dstp, lk, lq, start=True, stop=True),
                            reads=[lk, lq], writes=[dstp])

                def d_expav(p, q0, pO):
                    b0 = (p % 2) * 1024
                    pt = PTD[:, p % 2, :]
                    sb.op("scalar", lambda e, pt=pt, b0=b0: e.activation(
                        pt, self.pD[:, b0:b0 + 1024], AF.Exp, scale=0.125),
                        reads=[self.pD[:, b0:b0 + 1024]], writes=[pt])
                    for o in range(2):
                        kc = 2 * p + o
                        lv = V_hp[:, kc, hh, :]
                        rp = PTD[:, p % 2, o * 512:o * 512 + 512]
                        sb.op("tensor", lambda e, lv=lv, rp=rp, pO=pO, kc=kc: e.matmul(
                            pO[0:65, 0:512], lv, rp, start=(kc == 0), stop=(kc == 31)),
                            reads=[lv, rp], writes=[pO[0:65, 0:512]])

                for qb in range(4):
                    q0 = qb * 512
                    pO = self.ps4 if qb % 2 == 0 else self.ps5
                    prevp = None
                    for p in range(16):
                        d_scores(p, q0)
                        if prevp is not None:
                            d_expav(prevp, q0, pO)
                        prevp = p
                    d_expav(prevp, q0, pO)
                    rr = rcp[0:1, qb % 2, 0:512]
                    sb.op("scalar", lambda e, rr=rr, pO=pO: e.copy(rr, pO[64:65, 0:512]),
                          reads=[pO[64:65, 0:512]], writes=[rr])
                    osv = rcp[0:64, qb % 2, 1024:1536]
                    sb.op("scalar", lambda e, osv=osv, pO=pO: e.copy(osv, pO[0:64, 0:512]),
                          reads=[pO[0:64, 0:512]], writes=[osv])
                    sb.op("vector", lambda e, rr=rr: e.reciprocal(rr, rr),
                          reads=[rr], writes=[rr])
                    pBC = self.prow
                    sb.op("tensor", lambda e, rr=rr, pBC=pBC: e.matmul(
                        pBC[0:64, 0:512], self.onesrow0[0:1, 0:64], rr,
                        start=True, stop=True),
                        reads=[self.onesrow0[0:1, 0:64], rr], writes=[pBC[0:64, 0:512]])
                    bcs = rcp[0:64, qb % 2, 512:1024]
                    sb.op("scalar", lambda e, bcs=bcs, pBC=pBC: e.copy(bcs, pBC[0:64, 0:512]),
                          reads=[pBC[0:64, 0:512]], writes=[bcs])
                    oso = ost[0:64, qb % 2, :]
                    sb.op("gpsimd", lambda e, bcs=bcs, osv=osv, oso=oso: e.tensor_tensor(
                        oso, osv, bcs, ALU.mult),
                        reads=[osv, bcs], writes=[oso])
                    sb.dma("sync", dm["Ocm"][h * 64:(h + 1) * 64, q0:q0 + 512], oso)
        self.ar.pop()

    def final_out(self):
        sb, dm = self.sb, self.dm
        self.ar.push()
        xc = self.ar.take([128, 2, 512], F32)
        x3c8 = self.ar.take([128, NCH, 512], F32)
        stg = self.ar.take([128, 2, D], F32)
        pT = self.prow
        for ci in range(4):
            c0 = ci * 512
            for ch in range(NCH):
                db = ch % 2
                sb.dma("sync", xc[:, db, :], dm["x2cm"][ch * 128:(ch + 1) * 128, c0:c0 + 512])
                dst = x3c8[:, ch, :]
                src2 = self.xt[:, ch, 1 + c0:1 + c0 + 512]
                sb.op("vector", lambda e, dst=dst, src2=src2, db=db: e.scalar_tensor_tensor(
                    dst, src2, 0.5, xc[:, db, :], ALU.mult, ALU.add),
                    reads=[src2, xc[:, db, :]], writes=[dst])
            for tt in range(4):
                t = ci * 4 + tt
                sg = stg[:, tt % 2, :]
                for ch in range(NCH):
                    src = x3c8[:, ch, tt * 128:(tt + 1) * 128]
                    sb.op("tensor", lambda e, src=src: e.transpose(
                        pT[:, 0:128], src, self.identf[:, :]),
                        reads=[src, self.identf[:, :]], writes=[pT[:, 0:128]])
                    dstc = sg[:, ch * 128:(ch + 1) * 128]
                    sb.op("vector", lambda e, dstc=dstc: e.tensor_copy(dstc, pT[:, 0:128]),
                          reads=[pT[:, 0:128]], writes=[dstc])
                sb.dma("sync", self.out[t * 128:(t + 1) * 128, :], sg[:, 0:D])
        self.ar.pop()

    def final_ci(self, ci, fxc, fstg):
        sb, dm = self.sb, self.dm
        pT = self.prow
        for tt in range(4):
            t = ci * 4 + tt
            for ch in range(NCH):
                sb.dma("sync" if ch % 2 else "scalar", fxc[:, ch, :],
                       dm["x2cm"][ch * 128:(ch + 1) * 128, t * 128:(t + 1) * 128])
            src2 = self.xt[:, :, 1 + t * 128:1 + (t + 1) * 128]
            sb.op("vector", lambda e, src2=src2: e.scalar_tensor_tensor(
                fxc, src2, 0.5, fxc, ALU.mult, ALU.add),
                reads=[src2, fxc], writes=[fxc])
            for half in range(2):
                for k in range(4):
                    srcb = fxc[:, half * 4 + k, :]
                    sb.op("tensor", lambda e, srcb=srcb, k=k: e.transpose(
                        pT[:, k * 128:(k + 1) * 128], srcb, self.identf[:, :]),
                        reads=[srcb, self.identf[:, :]],
                        writes=[pT[:, k * 128:(k + 1) * 128]])
                dsth = fstg[:, 0:512]
                sb.op("vector", lambda e, dsth=dsth: e.tensor_copy(dsth, pT[:, 0:512]),
                      reads=[pT[:, 0:512]], writes=[dsth])
                sb.dma("sync", self.out[t * 128:(t + 1) * 128,
                                        half * 512:(half + 1) * 512], dsth)

    def _emit_out_from_xt(self):
        sb = self.sb
        pT = self.prow
        stg = self.stat
        for t in range(NT):
            for ch in range(NCH):
                src = self.xt[:, ch, 1 + t * 128:1 + (t + 1) * 128]
                sb.op("tensor",
                      lambda e, src=src: e.transpose(pT[:, 0:128], src, self.identf[:, :]),
                      reads=[src, self.identf[:, :]], writes=[pT[:, 0:128]])
                dst = stg[:, ch * 128:(ch + 1) * 128]
                sb.op("vector", lambda e, dst=dst: e.tensor_copy(dst, pT[:, 0:128]),
                      reads=[pT[:, 0:128]], writes=[dst])
            sb.dma("sync", self.out[t * 128:(t + 1) * 128, :], stg[:, 0:D])



def _in_maps(L0_attn_z, L0_attn_W_qkv, L0_ffn_wqkv, L0_ffn_q_scale, L0_ffn_k_scale,
             L0_ffn_wo, L0_ffn_ln_g, L0_ffn_ln_b, L0_ffn_w1, L0_ffn_b1, L0_ffn_w2,
             L0_ffn_b2):
    maps = []
    for c in range(NCORES):
        bat, half = c // 2, c % 2
        m = {
            "x0": np.ascontiguousarray(np.asarray(L0_attn_z)[bat, half * T:(half + 1) * T, :],
                                       dtype=np.float32),
            "Wd": np.ascontiguousarray(L0_attn_W_qkv, dtype=np.float32),
            "wqkv": np.ascontiguousarray(L0_ffn_wqkv, dtype=np.float32),
            "qs": np.ascontiguousarray(L0_ffn_q_scale, dtype=np.float32),
            "ks": np.ascontiguousarray(L0_ffn_k_scale, dtype=np.float32),
            "wo": np.ascontiguousarray(L0_ffn_wo, dtype=np.float32),
            "lng": np.ascontiguousarray(L0_ffn_ln_g, dtype=np.float32),
            "lnb": np.ascontiguousarray(L0_ffn_ln_b, dtype=np.float32),
            "w1": np.ascontiguousarray(L0_ffn_w1, dtype=np.float32),
            "b1": np.ascontiguousarray(L0_ffn_b1, dtype=np.float32),
            "w2": np.ascontiguousarray(L0_ffn_w2, dtype=np.float32),
            "b2": np.ascontiguousarray(L0_ffn_b2, dtype=np.float32),
        }
        m.update(_host_tables(half))
        maps.append(m)
    return maps


def kernel(**inputs):
    b = Build().build()
    maps = _in_maps(**inputs)
    res = run_bass_kernel_spmd(b.nc, maps, core_ids=list(range(NCORES)))
    full = np.zeros((B, N, D), np.float32)
    for c in range(NCORES):
        bat, half = c // 2, c % 2
        full[bat, half * T:(half + 1) * T, :] = np.asarray(res.results[c]["out"])
    return full
